# version 07
# baseline (speedup 1.0000x reference)
import numpy as np
import ml_dtypes
import concourse.bass as bass
import concourse.mybir as mybir
from concourse.bass_utils import run_bass_kernel_spmd
from contextlib import ExitStack

F32 = mybir.dt.float32
BF16 = mybir.dt.bfloat16
AF = mybir.ActivationFunctionType
ALU = mybir.AluOpType
AX = mybir.AxisListType

NCORES = 8
P = 128
DM = 1024
H = 16
DHD = 64
FF = 4096
FCN = 32
LN_EPS = 1e-5
ATT_EPS = 1e-6
E_LOC = 2
ES = 1024

SZ_PROJ = DM * DM
OFF_WQ = 0
OFF_WK = SZ_PROJ
OFF_WV = 2 * SZ_PROJ
OFF_WO = 3 * SZ_PROJ
OFF_W1 = 4 * SZ_PROJ
OFF_W2 = OFF_W1 + DM * FF
LAYER_ELEMS = OFF_W2 + DM * FF
SHARD_COLS = LAYER_ELEMS // NCORES // P

ENGINES = ("tensor", "vector", "scalar", "gpsimd", "sync")
SEM_ROTATE = 24000
NDMA_SEMS = 10


class _Op:
    __slots__ = ("eng", "emit", "reads", "writes", "dma", "coll", "idx",
                 "deps", "inc", "sem", "val", "tag", "pre_wait", "fence_f")


class Prog:
    def __init__(self, nc):
        self.nc = nc
        self.ops = []
        self.wr = {}
        self.rd = {}

    def op(self, eng, emit, reads=(), writes=(), dma=False, coll=False, tag="",
           fence_f=False):
        o = _Op()
        o.eng, o.emit, o.dma, o.coll, o.tag = eng, emit, dma, coll, tag
        o.fence_f = fence_f
        o.reads, o.writes = tuple(reads), tuple(writes)
        o.idx = len(self.ops)
        deps = set()
        for k in o.reads:
            if k in self.wr:
                deps.add(self.wr[k])
        for k in o.writes:
            if k in self.wr:
                deps.add(self.wr[k])
            for r in self.rd.get(k, ()):
                deps.add(r)
        deps.discard(o.idx)
        o.deps = deps
        for k in o.reads:
            self.rd.setdefault(k, []).append(o.idx)
        for k in o.writes:
            self.wr[k] = o.idx
            self.rd[k] = []
        self.ops.append(o)
        return o.idx

    def t(self, emit, reads=(), writes=(), tag=""):
        return self.op("tensor", emit, reads, writes, tag=tag)

    def v(self, emit, reads=(), writes=(), tag=""):
        return self.op("vector", emit, reads, writes, tag=tag)

    def s(self, emit, reads=(), writes=(), tag=""):
        return self.op("scalar", emit, reads, writes, tag=tag)

    def g(self, emit, reads=(), writes=(), tag=""):
        return self.op("gpsimd", emit, reads, writes, tag=tag)

    def dma(self, queue, emit, reads=(), writes=(), tag=""):
        return self.op(queue, emit, reads, writes, dma=True, tag=tag)

    def coll(self, emit, reads=(), writes=(), tag=""):
        return self.op("gpsimd", emit, reads, writes, coll=True, tag=tag)

    def fence(self, keys, reads=(), tag="fence"):
        return self.op("vector", lambda e: e.nop(nofuse=True), reads, keys, tag=tag,
                       fence_f=True)

    def finalize(self, ctx):
        nc = self.nc
        ops = self.ops
        for o in ops:
            pruned = set()
            for d in o.deps:
                od = ops[d]
                if (od.eng == o.eng == "tensor" and not od.dma and not od.coll
                        and not o.dma and not o.coll):
                    continue
                pruned.add(d)
            same = [d for d in pruned
                    if ops[d].eng == o.eng and not ops[d].dma and not ops[d].coll]
            if len(same) > 1:
                keep = max(same)
                for d in same:
                    if d != keep:
                        pruned.discard(d)
            o.deps = pruned
        need = set()
        for o in ops:
            need |= o.deps
        counters = {e: 0 for e in ENGINES}
        sem_pool = {e: [] for e in ENGINES}
        sem_base = {e: [] for e in ENGINES}

        def cur_sem(eng):
            if not sem_pool[eng] or counters[eng] - sem_base[eng][-1] > SEM_ROTATE:
                h = ctx.enter_context(nc.semaphore(f"s_{eng}_{len(sem_pool[eng])}"))
                sem_pool[eng].append(h)
                sem_base[eng].append(counters[eng])
            return sem_pool[eng][-1], sem_base[eng][-1]

        dma_pool = {}
        dma_rr = {}
        for o in ops:
            o.pre_wait = None
            if o.idx in need or o.coll or o.dma:
                if o.coll or o.fence_f:
                    h = ctx.enter_context(nc.semaphore(f"s_coll_{o.idx}"))
                    o.sem, o.inc, o.val = h, 1, 1
                elif o.dma:
                    pool = dma_pool.setdefault(o.eng, [])
                    if len(pool) < NDMA_SEMS:
                        h = ctx.enter_context(nc.semaphore(f"sd_{o.eng}_{len(pool)}"))
                        pool.append([h, 0])
                        slot = pool[-1]
                    else:
                        i = dma_rr.get(o.eng, 0)
                        slot = pool[i % NDMA_SEMS]
                        dma_rr[o.eng] = i + 1
                    if slot[1] > 0:
                        o.pre_wait = (slot[0], 16 * slot[1])
                    slot[1] += 1
                    o.sem, o.inc, o.val = slot[0], 16, 16 * slot[1]
                else:
                    o.inc = 1
                    h, base = cur_sem(o.eng)
                    counters[o.eng] += 1
                    o.sem, o.val = h, counters[o.eng] - base
            else:
                o.sem = None

    def emit_all(self, block):
        ops = self.ops
        for eng_name in ENGINES:
            mine = [o for o in ops if o.eng == eng_name]
            if not mine:
                continue

            def body(eng, mine=mine):
                watermark = {}
                for o in mine:
                    if o.pre_wait is not None:
                        sem, val = o.pre_wait
                        if watermark.get(id(sem), -1) < val:
                            eng.wait_ge(sem, val)
                            watermark[id(sem)] = val
                    waits = {}
                    for d in o.deps:
                        od = ops[d]
                        if od.sem is None:
                            continue
                        sid = id(od.sem)
                        if waits.get(sid, (None, -1))[1] < od.val:
                            waits[sid] = (od.sem, od.val)
                    for sem, val in waits.values():
                        sid = id(sem)
                        if watermark.get(sid, -1) < val:
                            eng.wait_ge(sem, val)
                            watermark[sid] = val
                    inst = o.emit(eng)
                    if o.sem is not None:
                        if inst is None:
                            raise RuntimeError(f"op {o.idx} {o.tag}: needs inc, emit->None")
                        inst.then_inc(o.sem, o.inc)
            getattr(block, eng_name)(body)

    def run(self, ctx, block):
        self.finalize(ctx)
        self.emit_all(block)



def build_program(T=2048, NL=6, debug=(), stop_after=None, skips=()):
    NCH = T // P
    GSZ = 4
    NGT = NCH // GSZ
    VKEYS = [f"v{tc}" for tc in range(NCH)]
    nc = bass.Bass("TRN2", target_bir_lowering=False, debug=False)

    x_d = nc.dram_tensor("x", [T, DM], F32, kind="ExternalInput").ap()
    wsh_d = nc.dram_tensor("wsh", [NL, P, SHARD_COLS], F32, kind="ExternalInput").ap()
    bq_d = nc.dram_tensor("bq", [NL, P, 8], F32, kind="ExternalInput").ap()
    bk_d = nc.dram_tensor("bk", [NL, P, 8], F32, kind="ExternalInput").ap()
    b1c_d = nc.dram_tensor("b1cx", [NL, P, 32], F32, kind="ExternalInput").ap()
    rows_d = nc.dram_tensor("rows", [NL, 7, DM], F32, kind="ExternalInput").ap()
    orows_d = nc.dram_tensor("orows", [8, DM], F32, kind="ExternalInput").ap()
    scale_d = nc.dram_tensor("scalex", [1, 1], F32, kind="ExternalInput").ap()
    sed_d = nc.dram_tensor("sed", [ES, DM], F32, kind="ExternalInput").ap()
    wg_d = nc.dram_tensor("wgx", [E_LOC, 8, P, 2048], F32, kind="ExternalInput").ap()
    wu_d = nc.dram_tensor("wux", [E_LOC, 8, P, 2048], F32, kind="ExternalInput").ap()
    wd_d = nc.dram_tensor("wdx", [E_LOC, 16, P, DM], F32, kind="ExternalInput").ap()
    bg_d = nc.dram_tensor("bgx", [E_LOC, 2048], F32, kind="ExternalInput").ap()
    bu_d = nc.dram_tensor("bux", [E_LOC, 2048], F32, kind="ExternalInput").ap()
    bd_d = nc.dram_tensor("bdx", [E_LOC, DM], F32, kind="ExternalInput").ap()
    maskU_d = nc.dram_tensor("maskU", [P, P], F32, kind="ExternalInput").ap()
    ident_d = nc.dram_tensor("identx", [P, P], F32, kind="ExternalInput").ap()
    rmask_d = nc.dram_tensor("rmaskx", [P, NCORES], F32, kind="ExternalInput").ap()
    out_d = nc.dram_tensor("out", [T, DM], F32, kind="ExternalOutput").ap()
    dbg_d = {}
    _DBG_SHAPES = {"qg": [P, 4096], "kg": [P, 4096], "vv": [P, DM],
                   "att0": [P, 2048], "den0": [P, H], "fmt": [P, 4096],
                   "kvb": [P, 520], "wqb": [P, DM]}
    for tagn in debug:
        dbg_d[tagn] = nc.dram_tensor(f"dbg_{tagn}", _DBG_SHAPES.get(tagn, [T, DM]),
                                     F32, kind="ExternalOutput").ap()

    wbf_sh = [nc.dram_tensor(f"wbf_sh{l}", [P, SHARD_COLS], BF16).ap() for l in range(NL)]
    wgth = [nc.dram_tensor(f"wgth{l}", [NCORES * P, SHARD_COLS], BF16).ap() for l in range(NL)]
    kvt = [nc.dram_tensor(f"kvt{l}", [P, 520], F32).ap() for l in range(NL)]
    kvg = [nc.dram_tensor(f"kvg{l}", [NCORES * P, 520], F32).ap() for l in range(NL)]
    mx_in = nc.dram_tensor("mx_in", [8, P], F32).ap()
    mx_out = nc.dram_tensor("mx_out", [8, P], F32).ap()
    z_in = nc.dram_tensor("z_in", [NCORES, P], F32).ap()
    z_rs = nc.dram_tensor("z_rs", [1, P], F32).ap()
    sl_in = nc.dram_tensor("sl_in", [ES, DM], F32).ap()
    sl_rs = nc.dram_tensor("sl_rs", [P, DM], F32).ap()
    disp_dr = nc.dram_tensor("disp_dr", [T, ES], BF16).ap()
    xh_dr = nc.dram_tensor("xh_dr", [T, DM], BF16).ap()
    ct_dr = nc.dram_tensor("ct_dr", [8, P, T], BF16).ap()
    eo_in = nc.dram_tensor("eo_in", [P, DM], BF16).ap()
    eo_out = nc.dram_tensor("eo_out", [ES, DM], BF16).ap()
    wgbf = nc.dram_tensor("wgbf", [E_LOC, 8, P, 2048], BF16).ap()
    wubf = nc.dram_tensor("wubf", [E_LOC, 8, P, 2048], BF16).ap()
    wdbf = nc.dram_tensor("wdbf", [E_LOC, 16, P, DM], BF16).ap()

    def wslice(l, off, n, cols):
        flat = wgth[l].rearrange("a b -> (a b)")
        return flat[off:off + n].rearrange("(p q) -> p q", q=cols)

    with ExitStack() as ctx:
        sbt = lambda n, s, d: ctx.enter_context(nc.sbuf_tensor(n, s, d))
        pst = lambda n, s, d: ctx.enter_context(nc.psum_tensor(n, s, d))
        pr = Prog(nc)

        stream = sbt("stream", [P, NCH * DM], F32)
        A32 = sbt("A32", [P, 16384], BF16)
        B16 = sbt("B16", [P, 8192], BF16)
        fmT = sbt("fmT", [P, 4096], BF16)
        att = sbt("att", [P, 2048], BF16)
        a_tok = sbt("a_tok", [P, DM], BF16)
        aT = sbt("aT", [P, DM], BF16)
        ktokb = sbt("ktokb", [P, DM], BF16)
        kv_sb = sbt("kv_sb", [P, 520], F32)
        kv_bf = sbt("kv_bf", [P, 520], BF16)
        tots = sbt("tots", [P, 520], F32)
        gtile = sbt("gtile", [P, 520], F32)
        wqb = [sbt(f"wqb{i}", [P, DM], BF16) for i in range(2)]
        wkb = [sbt(f"wkb{i}", [P, DM], BF16) for i in range(2)]
        wvb = sbt("wvb", [P, 4096], BF16)
        wob = [sbt(f"wob{i}", [P, DM], BF16) for i in range(2)]
        w1b = [sbt(f"w1b{i}", [P, DM], BF16) for i in range(2)]
        w2b = [sbt(f"w2b{i}", [P, DM], BF16) for i in range(2)]
        bc = sbt("bc", [P, 4096], BF16)
        bqc = sbt("bqc", [P, 8], F32)
        bkc = sbt("bkc", [P, 8], F32)
        b1c = sbt("b1c", [P, 32], F32)
        scr1 = sbt("scr1", [P, 512], F32)
        scr2 = sbt("scr2", [P, 512], F32)
        scrw = sbt("scrw", [P, DM], F32)
        bnst = sbt("bnst", [P, 12], F32)
        mv = sbt("mv", [P, 2], F32)
        nm = sbt("nm", [P, 1], F32)
        rstd = sbt("rstd", [P, 1], F32)
        den = sbt("den", [P, H], F32)
        rden = sbt("rden", [P, H], F32)
        ztmp = sbt("ztmp", [P, 1], F32)
        maskU = sbt("maskUs", [P, P], BF16)
        mask4 = sbt("mask4", [P, 512], BF16)
        ident = sbt("idents", [P, P], BF16)
        onesb = sbt("onesb", [P, 1], BF16)
        rmask = sbt("rmasks", [P, NCORES], F32)
        zcol = sbt("zcol", [P, 8], F32)
        dspb = sbt("dspb", [P, ES], BF16)
        xhb = sbt("xhb", [P, DM], BF16)
        slst = sbt("slst", [P, DM], F32)
        rowmax = sbt("rowmax", [P, 1], F32)
        rowsum = sbt("rowsum", [P, 1], F32)
        gub = sbt("gub", [P, 2048], BF16)
        upb = sbt("upb", [P, 2048], F32)
        scl = sbt("scl", [P, 1], F32)
        ceps = sbt("ceps", [P, 1], F32)

        pb = [pst(f"pb{i}", [P, 512], F32) for i in range(8)]
        pbh = [b.bitcast(BF16) for b in pb]

        st = lambda tc: stream[:, tc * DM:(tc + 1) * DM]

        def bcast_fill(slot, src_row_ap, key, n=DM, tag="bc"):
            pr.dma("gpsimd", lambda e: e.dma_start(
                out=bc[:, slot * DM: slot * DM + n],
                in_=src_row_ap.broadcast_to([P, n])),
                writes=[key], tag=tag)

        def rsqrt_of(src_ap, rkey):
            pr.s(lambda e: e.activation(rstd[:, :], src_ap, AF.Ln, bias=ceps[:, 0:1]),
                 reads=[rkey, "rstd", "ceps"], writes=["rstd"], tag="lnv")
            pr.s(lambda e: e.activation(rstd[:, :], rstd[:, :], AF.Exp, scale=-0.5),
                 reads=["rstd"], writes=["rstd"], tag="expv")

        def ln_stats(src_ap, rkeys):
            pr.v(lambda e: e.bn_stats(bnst[:, 0:6], src_ap[:, 0:512]),
                 reads=list(rkeys), writes=["bnst"], tag="bns")
            pr.v(lambda e: e.bn_stats(bnst[:, 6:12], src_ap[:, 512:1024]),
                 reads=list(rkeys) + ["bnst"], writes=["bnst"], tag="bns")
            pr.v(lambda e: e.bn_aggr(mv[:, :], bnst[:, :]),
                 reads=["bnst"], writes=["mv"], tag="bna")
            pr.v(lambda e: e.tensor_scalar(nm[:, :], mv[:, 0:1], -1.0, None, ALU.mult),
                 reads=["mv"], writes=["nm"], tag="nm")
            rsqrt_of(mv[:, 1:2], "mv")

        def ln_tile(tc, gslot, bslot, gkey, bkey, tag="ln"):
            s_ = st(tc)
            k = f"st{tc}"
            ln_stats(s_, [k])
            pr.v(lambda e: e.tensor_scalar(s_, s_, nm[:, 0:1], rstd[:, 0:1], ALU.add, ALU.mult),
                 reads=[k, "nm", "rstd"], writes=[k], tag=tag)
            if gslot is not None:
                pr.v(lambda e: e.tensor_tensor(s_, s_, bc[:, gslot * DM:(gslot + 1) * DM], ALU.mult),
                     reads=[k, gkey], writes=[k], tag=tag)
                pr.v(lambda e: e.tensor_tensor(s_, s_, bc[:, bslot * DM:(bslot + 1) * DM], ALU.add),
                     reads=[k, bkey], writes=[k], tag=tag)

        def transpose8(src, rkeys, dst_ap, wkey, bank=7, tag="tr", ev_in=None):
            for dkx in range(8):
                sap = src(dkx) if callable(src) else src[:, dkx * P:(dkx + 1) * P]
                pr.t(lambda e, sap=sap, dkx=dkx: e.transpose(
                    pbh[bank][:, dkx * P:(dkx + 1) * P], sap, ident[:, :]),
                    reads=list(rkeys) + ["ident"], writes=[f"pb{bank}"], tag=tag)
            inap = ev_in if ev_in is not None else pbh[bank][:, :]
            pr.v(lambda e: e.tensor_copy(dst_ap, inap),
                 reads=[f"pb{bank}"], writes=[wkey], tag=tag + "ev")

        def fm_evac(bank, bias_ap, out_ap, rkeys, wkeys, tag="fm"):
            pr.s(lambda e: e.activation(scr1[:, :], pb[bank][:, :], AF.Relu, bias=bias_ap),
                 reads=list(rkeys) + ["scr1"], writes=["scr1"], tag=tag)
            pr.v(lambda e: e.tensor_scalar(scr2[:, :], pb[bank][:, :], bias_ap, 0.0, ALU.add, ALU.min),
                 reads=list(rkeys) + ["scr2", "scr1"], writes=["scr2"], tag=tag)
            pr.s(lambda e: e.activation(scr2[:, :], scr2[:, :], AF.Exp),
                 reads=["scr2"], writes=["scr2"], tag=tag)
            pr.v(lambda e: e.tensor_tensor(out_ap, scr1[:, :], scr2[:, :], ALU.add),
                 reads=["scr1", "scr2"], writes=list(wkeys), tag=tag)

        pr.dma("gpsimd", lambda e: e.dma_start(out=maskU[:, :], in_=maskU_d), writes=["maskU"])
        pr.dma("gpsimd", lambda e: e.dma_start(out=ident[:, :], in_=ident_d), writes=["ident"])
        pr.dma("sync", lambda e: e.dma_start(out=rmask[:, :], in_=rmask_d), writes=["rmask"])
        for i in range(4):
            pr.v(lambda e, i=i: e.tensor_copy(mask4[:, i * P:(i + 1) * P], maskU[:, :]),
                 reads=["maskU", "mask4"], writes=["mask4"])
        pr.v(lambda e: e.memset(onesb[:, :], 1.0), writes=["onesb"])
        pr.v(lambda e: e.memset(ceps[:, :], LN_EPS), writes=["ceps"])

        def emit_gather(l):
            pr.coll(lambda e, l=l: e.collective_compute(
                "AllGather", ALU.bypass, replica_groups=[list(range(NCORES))],
                ins=[wbf_sh[l].opt()], outs=[wgth[l].opt()]),
                reads=[f"wbf{l}"], writes=[f"wg{l}"], tag=f"gath{l}")

        for l in range(NL):
            pr.dma("gpsimd", lambda e, l=l: e.dma_start(
                out=wbf_sh[l].rearrange("p (a q) -> p a q", q=2048),
                in_=wsh_d[l].rearrange("p (a q) -> p a q", q=2048)),
                writes=[f"wbf{l}"], tag=f"cast{l}")
        for l in range(min(2, NL)):
            emit_gather(l)

        for tc in range(NCH):
            pr.dma("sync", lambda e, tc=tc: e.dma_start(out=st(tc), in_=x_d[tc * P:(tc + 1) * P, :]),
                   writes=[f"st{tc}"], tag="ldx")
        for tc in range(NCH):
            ln_tile(tc, None, None, None, None, tag="lnxn")
        if "xn" in debug:
            for tc in range(NCH):
                pr.dma("sync", lambda e, tc=tc: e.dma_start(
                    out=dbg_d["xn"][tc * P:(tc + 1) * P, :], in_=st(tc)),
                    reads=[f"st{tc}"], writes=["dbgk"], tag="dbg")

        bcast_fill(0, orows_d[4:5, :], "bc0", tag="sg")
        bcast_fill(1, orows_d[5:6, :], "bc1", tag="sb")
        pr.dma("gpsimd", lambda e: e.dma_start(out=scl[:, :], in_=scale_d.broadcast_to([P, 1])),
               writes=["scl"], tag="scl")
        for c in range(8):
            pr.dma("sync", lambda e, c=c: e.dma_start(out=scrw[:, :], in_=sed_d[c * P:(c + 1) * P, :]),
                   reads=["scrw"], writes=["scrw"], tag="ldse")
            ln_stats(scrw[:, :], ["scrw"])
            pr.v(lambda e: e.tensor_scalar(scrw[:, :], scrw[:, :], nm[:, 0:1], rstd[:, 0:1], ALU.add, ALU.mult),
                 reads=["scrw", "nm", "rstd"], writes=["scrw"], tag="selin")
            pr.v(lambda e: e.tensor_tensor(scrw[:, :], scrw[:, :], bc[:, 0:DM], ALU.mult),
                 reads=["scrw", "bc0"], writes=["scrw"], tag="seg")
            pr.v(lambda e: e.tensor_tensor(scrw[:, :], scrw[:, :], bc[:, DM:2 * DM], ALU.add),
                 reads=["scrw", "bc1"], writes=["scrw"], tag="seb")
            pr.v(lambda e: e.tensor_scalar(xhb[:, :], scrw[:, :], scl[:, 0:1], None, ALU.mult),
                 reads=["scrw", "scl", "xhb"], writes=["xhb"], tag="sescale")
            transpose8(xhb[:, :], ["xhb"],
                       B16[:, :].rearrange("p (dk q) -> p dk q", q=1024)[:, :, c * P:(c + 1) * P],
                       "seT", bank=7, tag="trse",
                       ev_in=pbh[7][:, :].rearrange("p (dk q) -> p dk q", q=P))

        pr.v(lambda e: e.memset(zcol[:, :], -3.0e38), writes=["zcol"], tag="gmax0")
        bcast_fill(2, orows_d[2:3, :], "bc2", tag="ng")
        bcast_fill(3, orows_d[3:4, :], "bc3", tag="nb")

        def make_xhat(tc, store):
            s_ = st(tc)
            k = f"st{tc}"
            ln_stats(s_, [k])
            pr.v(lambda e: e.tensor_scalar(scrw[:, :], s_, nm[:, 0:1], rstd[:, 0:1], ALU.add, ALU.mult),
                 reads=[k, "nm", "rstd", "scrw"], writes=["scrw"], tag="xh1")
            pr.v(lambda e: e.tensor_tensor(scrw[:, :], scrw[:, :], bc[:, 2 * DM:3 * DM], ALU.mult),
                 reads=["scrw", "bc2"], writes=["scrw"], tag="xh2")
            pr.v(lambda e: e.tensor_tensor(xhb[:, :], scrw[:, :], bc[:, 3 * DM:4 * DM], ALU.add),
                 reads=["scrw", "bc3", "xhb"], writes=["xhb"], tag="xh3")
            if store:
                pr.dma("gpsimd", lambda e, tc=tc: e.dma_start(
                    out=xh_dr[tc * P:(tc + 1) * P, :], in_=xhb[:, :]),
                    reads=["xhb"], writes=["xh_dr"], tag="st_xh")

        def logits_psum(tc):
            transpose8(xhb[:, :], ["xhb"], a_tok[:, :], "a_tok", bank=6, tag="trxh")
            for half in range(2):
                for dkx in range(8):
                    pr.t(lambda e, half=half, dkx=dkx: e.matmul(
                        pb[half][:, :], a_tok[:, dkx * P:(dkx + 1) * P],
                        B16[:, dkx * 1024 + half * 512: dkx * 1024 + (half + 1) * 512],
                        start=(dkx == 0), stop=(dkx == 7)),
                        reads=["a_tok", "seT"], writes=[f"pb{half}"], tag="mmlog")

        for tc in range(NCH):
            make_xhat(tc, store=True)
            logits_psum(tc)
            for half in range(2):
                pr.v(lambda e, half=half: e.tensor_copy(
                    dspb[:, half * 512:(half + 1) * 512], pb[half][:, :]),
                    reads=[f"pb{half}", "dspb"], writes=["dspb"], tag="logbf")
            for c in range(8):
                sl7 = c % 8
                pr.t(lambda e, c=c, sl7=sl7: e.transpose(
                    pbh[7][:, sl7 * P:(sl7 + 1) * P], dspb[:, c * P:(c + 1) * P], ident[:, :]),
                    reads=["dspb", "ident"], writes=["pb7"], tag="trlog")
                pr.v(lambda e, sl7=sl7: e.reduce_max(ztmp[:, :], pbh[7][:, sl7 * P:(sl7 + 1) * P], axis=AX.X),
                     reads=["pb7", "ztmp"], writes=["ztmp"], tag="redmax")
                pr.v(lambda e, c=c: e.tensor_tensor(zcol[:, c:c + 1], zcol[:, c:c + 1], ztmp[:, :], ALU.max),
                     reads=["ztmp", "zcol"], writes=["zcol"], tag="accmax")
        for c in range(8):
            pr.dma("gpsimd", lambda e, c=c: e.dma_start(
                out=mx_in[c:c + 1, :].rearrange("a b -> (a b)").rearrange("(b a) -> b a", a=1),
                in_=zcol[:, c:c + 1]),
                reads=["zcol"], writes=["mx_in"], tag="stmax")
        pr.coll(lambda e: e.collective_compute(
            "AllReduce", ALU.max, replica_groups=[list(range(NCORES))],
            ins=[mx_in.opt()], outs=[mx_out.opt()]),
            reads=["mx_in"], writes=["mx_out"], tag="ar_max")
        gmax_row = mx_out.rearrange("a b -> (a b)").rearrange("(o n) -> o n", o=1)
        pr.dma("gpsimd", lambda e: e.dma_start(out=bc[:, 0:DM], in_=gmax_row.broadcast_to([P, ES])),
               reads=["mx_out", "bc0"], writes=["bc0"], tag="bcmax")

        for tc in range(NCH):
            make_xhat(tc, store=False)
            logits_psum(tc)
            pr.v(lambda e: e.reduce_max(rowmax[:, :], pb[0][:, :], axis=AX.X),
                 reads=["pb0", "rowmax"], writes=["rowmax"], tag="cmax")
            pr.v(lambda e: e.reduce_max(ztmp[:, :], pb[1][:, :], axis=AX.X),
                 reads=["pb1", "ztmp"], writes=["ztmp"], tag="cmax")
            pr.v(lambda e: e.tensor_tensor(rowmax[:, :], rowmax[:, :], ztmp[:, :], ALU.max),
                 reads=["rowmax", "ztmp"], writes=["rowmax"], tag="cmax")
            pr.v(lambda e: e.tensor_scalar(rowmax[:, :], rowmax[:, :], -1.0, None, ALU.mult),
                 reads=["rowmax"], writes=["rowmax"], tag="cneg")
            for half in range(2):
                pr.s(lambda e, half=half: e.activation(
                    scrw[:, half * 512:(half + 1) * 512], pb[half][:, :], AF.Exp, bias=rowmax[:, 0:1]),
                    reads=[f"pb{half}", "rowmax", "scrw"], writes=["scrw"], tag="cexp")
            pr.v(lambda e: e.reduce_sum(rowsum[:, :], scrw[:, :], axis=AX.X),
                 reads=["scrw"], writes=["rowsum"], tag="csum")
            pr.v(lambda e: e.reciprocal(rowsum[:, :], rowsum[:, :]),
                 reads=["rowsum"], writes=["rowsum"], tag="crec")
            pr.v(lambda e: e.tensor_scalar(xhb[:, :], scrw[:, :], rowsum[:, 0:1], None, ALU.mult),
                 reads=["scrw", "rowsum", "xhb"], writes=["xhb"], tag="cdiv")
            transpose8(xhb[:, :], ["xhb"], a_tok[:, :], "a_tok", bank=6, tag="trcb")
            for c in range(8):
                pr.dma("sync", lambda e, c=c, tc=tc: e.dma_start(
                    out=ct_dr[c, :, tc * P:(tc + 1) * P], in_=a_tok[:, c * P:(c + 1) * P]),
                    reads=["a_tok"], writes=["ct_dr"], tag="stct")
            for half in range(2):
                pr.v(lambda e, half=half: e.tensor_tensor(
                    scrw[:, half * 512:(half + 1) * 512], pb[half][:, :],
                    bc[:, half * 512:(half + 1) * 512], ALU.subtract),
                    reads=[f"pb{half}", "bc0", "scrw"], writes=["scrw"], tag="dsub")
            for half in range(2):
                pr.s(lambda e, half=half: e.activation(
                    dspb[:, half * 512:(half + 1) * 512], scrw[:, half * 512:(half + 1) * 512], AF.Exp),
                    reads=["scrw", "dspb"], writes=["dspb"], tag="dexp")
            pr.dma("gpsimd", lambda e, tc=tc: e.dma_start(
                out=disp_dr[tc * P:(tc + 1) * P, :], in_=dspb[:, :]),
                reads=["dspb"], writes=["disp_dr"], tag="stdsp")
            for half in range(2):
                pr.t(lambda e, half=half, tc=tc: e.matmul(
                    pb[4 + half][0:1, :], onesb[:, :], dspb[:, half * 512:(half + 1) * 512],
                    start=(tc == 0), stop=(tc == NCH - 1)),
                    reads=["dspb", "onesb"], writes=[f"pb{4 + half}"], tag="zmm")
        pr.v(lambda e: e.tensor_copy(slst[0:1, 0:512], pb[4][0:1, :]),
             reads=["pb4", "slst"], writes=["slst"], tag="zev")
        pr.v(lambda e: e.tensor_copy(slst[0:1, 512:1024], pb[5][0:1, :]),
             reads=["pb5", "slst"], writes=["slst"], tag="zev")
        pr.dma("gpsimd", lambda e: e.dma_start(
            out=z_in.rearrange("a b -> (a b)").rearrange("(o n) -> o n", o=1), in_=slst[0:1, 0:ES]),
            reads=["slst"], writes=["z_in"], tag="stz")
        pr.coll(lambda e: e.collective_compute(
            "ReduceScatter", ALU.add, replica_groups=[list(range(NCORES))],
            ins=[z_in.opt()], outs=[z_rs.opt()]),
            reads=["z_in"], writes=["z_rs"], tag="rs_z")

        for half8 in range(2):
            for tc in range(NCH):
                pr.dma("sync", lambda e, tc=tc: e.dma_start(
                    out=dspb[:, :], in_=disp_dr[tc * P:(tc + 1) * P, :]),
                    reads=["disp_dr", "dspb"], writes=["dspb"], tag="lddsp")
                pr.dma("sync", lambda e, tc=tc: e.dma_start(
                    out=xhb[:, :], in_=xh_dr[tc * P:(tc + 1) * P, :]),
                    reads=["xh_dr", "xhb"], writes=["xhb"], tag="ldxh")
                for ce in range(4):
                    c = half8 * 4 + ce
                    for half in range(2):
                        pr.t(lambda e, c=c, ce=ce, half=half, tc=tc: e.matmul(
                            pb[2 * ce + half][:, :], dspb[:, c * P:(c + 1) * P],
                            xhb[:, half * 512:(half + 1) * 512],
                            start=(tc == 0), stop=(tc == NCH - 1)),
                            reads=["dspb", "xhb"], writes=[f"pb{2 * ce + half}"], tag="slmm")
            for ce in range(4):
                c = half8 * 4 + ce
                for half in range(2):
                    pr.v(lambda e, ce=ce, half=half: e.tensor_copy(
                        slst[:, half * 512:(half + 1) * 512], pb[2 * ce + half][:, :]),
                        reads=[f"pb{2 * ce + half}", "slst"], writes=["slst"], tag="slev")
                pr.dma("gpsimd", lambda e, c=c: e.dma_start(
                    out=sl_in[c * P:(c + 1) * P, :], in_=slst[:, :]),
                    reads=["slst"], writes=["sl_in"], tag="stsl")
        pr.coll(lambda e: e.collective_compute(
            "ReduceScatter", ALU.add, replica_groups=[list(range(NCORES))],
            ins=[sl_in.opt()], outs=[sl_rs.opt()]),
            reads=["sl_in"], writes=["sl_rs"], tag="rs_sl")

        for l in range(2, NL):
            emit_gather(l)

        pr.dma("gpsimd", lambda e: e.dma_start(out=wgbf, in_=wg_d),
               writes=["wgbf"], tag="cast_wg")
        pr.dma("gpsimd", lambda e: e.dma_start(out=wubf, in_=wu_d),
               writes=["wubf"], tag="cast_wu")
        pr.dma("gpsimd", lambda e: e.dma_start(out=wdbf, in_=wd_d),
               writes=["wdbf"], tag="cast_wd")

        def trunc_finish():
            for tc in range(NCH):
                pr.dma("sync", lambda e, tc=tc: e.dma_start(
                    out=out_d[tc * P:(tc + 1) * P, :], in_=st(tc)),
                    reads=[f"st{tc}"], writes=["outd"], tag="stout")
            pr.op("sync", lambda e: e.nop(nofuse=True), reads=["outd", "dbgk"], tag="final")

        if stop_after == "outer1":
            trunc_finish()
            block = ctx.enter_context(nc.Block())
            pr.run(ctx, block)
            return nc

        k_g = B16[:, 4096:8192]
        q_g = B16[:, 0:4096]

        def fm_chunk(tc, j, bank):
            pr.s(lambda e, tc=tc: e.activation(a_tok[:, :], st(tc), AF.Copy, bias=0.0),
                 reads=[f"st{tc}", "a_tok"], writes=["a_tok"], tag="castst")
            transpose8(a_tok[:, :], ["a_tok"],
                       fmT[:, :].rearrange("p (dk q) -> p dk q", q=512)[:, :, j * P:(j + 1) * P],
                       "fmT", bank=bank, tag="trx",
                       ev_in=pbh[bank][:, :].rearrange("p (dk q) -> p dk q", q=P))

        def fmT_group(g, banks=(7, 7, 7, 7)):
            for j in range(GSZ):
                fm_chunk(g * GSZ + j, j, banks[j])

        for l in range(NL):
            lkey = f"wg{l}"
            pr.dma("sync", lambda e, l=l: e.dma_start(out=bqc[:, :], in_=bq_d[l]),
                   reads=["bqc"], writes=["bqc"], tag="ldbq")
            pr.dma("sync", lambda e, l=l: e.dma_start(out=bkc[:, :], in_=bk_d[l]),
                   reads=["bkc"], writes=["bkc"], tag="ldbk")
            pr.dma("sync", lambda e, l=l: e.dma_start(out=b1c[:, :], in_=b1c_d[l]),
                   reads=["b1c"], writes=["b1c"], tag="ldb1")

            def proj_feat(woff, wbufs, bias_cols, bkey, out_sb, outkeys, banks, tagp):
                for c8 in range(8):
                    bi = c8 % 2
                    bk_ = banks[bi]
                    wap = wslice(l, woff + c8 * SZ_PROJ // 8, SZ_PROJ // 8, DM)
                    pr.dma("sync", lambda e, wap=wap, bi=bi: e.dma_start(
                        out=wbufs[bi][:, :], in_=wap),
                        reads=[lkey, f"{tagp}b{bi}"], writes=[f"{tagp}b{bi}"], tag=f"ld{tagp}")
                    if "kmm" in skips:
                        pr.v(lambda e, bi=bi: e.tensor_copy(
                            out_sb[:, c8 * 512:(c8 + 1) * 512], wbufs[bi][:, 0:512]),
                            reads=[f"{tagp}b{bi}"], writes=outkeys, tag="skipev")
                        continue
                    for dkx in range(8):
                        pr.t(lambda e, c8=c8, dkx=dkx, bi=bi, bk_=bk_: e.matmul(
                            pb[bk_][:, :], wbufs[bi][:, dkx * P:(dkx + 1) * P],
                            fmT[:, dkx * 512:(dkx + 1) * 512],
                            start=(dkx == 0), stop=(dkx == 7)),
                            reads=[f"{tagp}b{bi}", "fmT"], writes=[f"pb{bk_}"], tag=f"mm{tagp}")
                    if "fmev" in skips:
                        pr.v(lambda e, bk_=bk_, c8=c8: e.tensor_copy(
                            out_sb[:, c8 * 512:(c8 + 1) * 512], pb[bk_][:, :]),
                            reads=[f"pb{bk_}"], writes=outkeys, tag="plainev")
                    else:
                        fm_evac(bk_, bias_cols[:, c8:c8 + 1],
                                out_sb[:, c8 * 512:(c8 + 1) * 512],
                                [f"pb{bk_}", bkey], outkeys, tag=f"fm{tagp}")

            bcast_fill(0, rows_d[l, 0:1, :], "bc0", tag="bv")
            pr.fence(VKEYS + ["hh"], tag="fv")
            for g in range(NGT):
                fmT_group(g)
                if "kproj" not in skips:
                    proj_feat(OFF_WK, wkb, bkc, "bkc", k_g, ["k_g"], (0, 1), "wk")
                for half in range(2 if "v" not in skips else 0):
                    wap = wslice(l, OFF_WV + half * SZ_PROJ // 2, SZ_PROJ // 2, 4096)
                    pr.dma("sync", lambda e, wap=wap: e.dma_start(
                        out=wvb[:, :], in_=wap),
                        reads=[lkey, "wvb"], writes=["wvb"], tag="ldwv")
                    for j in range(GSZ):
                        tc = g * GSZ + j
                        bk_ = 2 + (j % 2)
                        for dkx in range(8):
                            pr.t(lambda e, j=j, dkx=dkx, bk_=bk_: e.matmul(
                                pb[bk_][:, :],
                                fmT[:, dkx * 512 + j * P: dkx * 512 + (j + 1) * P],
                                wvb[:, dkx * 512:(dkx + 1) * 512],
                                start=(dkx == 0), stop=(dkx == 7)),
                                reads=["wvb", "fmT"], writes=[f"pb{bk_}"], tag="vmm")
                        pr.v(lambda e, tc=tc, half=half, bk_=bk_: e.tensor_tensor(
                            A32[:, tc * DM + half * 512: tc * DM + (half + 1) * 512],
                            pb[bk_][:, :], bc[:, half * 512:(half + 1) * 512], ALU.add),
                            reads=[f"pb{bk_}", "bc0", f"v{tc}"], writes=[f"v{tc}"], tag="vev")
                for j in range(GSZ if "kvtot" not in skips else 0):
                    tc = g * GSZ + j
                    transpose8(lambda dkx, j=j: k_g[:, dkx * 512 + j * P: dkx * 512 + (j + 1) * P],
                               ["k_g"], ktokb[:, :], "ktok", bank=7, tag="trkA")
                    for h in range(H):
                        c, rr = h // 2, (h % 2) * DHD
                        pr.t(lambda e, h=h, c=c, rr=rr, tc=tc: e.matmul(
                            pb[6][rr:rr + DHD, c * DHD:(c + 1) * DHD],
                            ktokb[:, c * P + rr: c * P + rr + DHD],
                            A32[:, tc * DM + h * DHD: tc * DM + (h + 1) * DHD],
                            start=(tc == 0 and c == 0), stop=(tc == NCH - 1 and h == H - 1),
                            skip_group_check=True),
                            reads=["ktok", f"v{tc}"], writes=["pb6"], tag="kvtot")
                    for c in range(8):
                        pr.v(lambda e, c=c, j=j: e.reduce_sum(
                            ztmp[:, :], k_g[:, c * 512 + j * P: c * 512 + (j + 1) * P], axis=AX.X),
                            reads=["k_g", "ztmp"], writes=["ztmp"], tag="zred")
                        if tc == 0:
                            pr.v(lambda e, c=c: e.tensor_copy(tots[:, 512 + c:513 + c], ztmp[:, :]),
                                 reads=["ztmp", "tots"], writes=["tots"], tag="zinit")
                        else:
                            pr.v(lambda e, c=c: e.tensor_tensor(
                                tots[:, 512 + c:513 + c], tots[:, 512 + c:513 + c], ztmp[:, :], ALU.add),
                                reads=["ztmp", "tots"], writes=["tots"], tag="zacc")
            if "kvtot" in skips:
                pr.v(lambda e: e.memset(tots[:, :], 0.0), reads=["tots"], writes=["tots"], tag="kvz")
            else:
                pr.v(lambda e: e.tensor_copy(tots[:, 0:512], pb[6][:, :]),
                     reads=["pb6", "tots"], writes=["tots"], tag="kvev")
            pr.dma("gpsimd", lambda e, l=l: e.dma_start(out=kvt[l], in_=tots[:, :]),
                   reads=["tots"], writes=[f"kvt{l}"], tag="stkv")
            if "kvcoll" not in skips:
                pr.coll(lambda e, l=l: e.collective_compute(
                    "AllGather", ALU.bypass, replica_groups=[list(range(NCORES))],
                    ins=[kvt[l].opt()], outs=[kvg[l].opt()]),
                    reads=[f"kvt{l}"], writes=[f"kvgd{l}"], tag=f"ag_kv{l}")
            pr.v(lambda e: e.memset(kv_sb[:, :], 0.0),
                 reads=["kv_bf"], writes=["kvsb"], tag="kv0")
            for r in range(NCORES if "kvcoll" not in skips else 0):
                pr.dma("sync", lambda e, r=r, l=l: e.dma_start(
                    out=gtile[:, :], in_=kvg[l][r * P:(r + 1) * P, :]),
                    reads=[f"kvgd{l}", "gtile"], writes=["gtile"], tag="ldkvg")
                pr.v(lambda e, r=r: e.scalar_tensor_tensor(
                    kv_sb[:, :], gtile[:, :], rmask[:, r:r + 1], kv_sb[:, :], ALU.mult, ALU.add),
                    reads=["gtile", "rmask", "kvsb"], writes=["kvsb"], tag="pref")

            if stop_after == f"passA{l + 1}":
                trunc_finish()
                block = ctx.enter_context(nc.Block())
                pr.run(ctx, block)
                return nc
            bcast_fill(1, rows_d[l, 1:2, :], "bc1", tag="bo")
            bcast_fill(2, rows_d[l, 3:4, :], "bc2", tag="g1")
            bcast_fill(3, rows_d[l, 4:5, :], "bc3", tag="be1")
            for g in range(NGT):
                fmT_group(g)
                proj_feat(OFF_WK, wkb, bkc, "bkc", k_g, ["k_g"], (0, 1), "wk")
                proj_feat(OFF_WQ, wqb, bqc, "bqc", q_g, ["q_g"], (2, 3), "wq")
                if l == 0 and g == 0 and "wqb" in debug:
                    pr.dma("gpsimd", lambda e: e.dma_start(out=dbg_d["wqb"], in_=wqb[0][:, :]),
                           reads=["wqb0"], writes=["dbgk"], tag="dbg")
                if l == 0 and g == 0:
                    for nm_, buf_, key_ in (("qg", q_g, "q_g"), ("kg", k_g, "k_g"),
                                            ("fmt", fmT[:, :], "fmT")):
                        if nm_ in debug:
                            pr.dma("gpsimd", lambda e, nm_=nm_, buf_=buf_: e.dma_start(
                                out=dbg_d[nm_], in_=buf_),
                                reads=[key_], writes=["dbgk"], tag="dbg")
                    if "vv" in debug:
                        pr.dma("gpsimd", lambda e: e.dma_start(
                            out=dbg_d["vv"], in_=A32[:, 0:DM]),
                            reads=["v0"], writes=["dbgk"], tag="dbg")
                for j in range(GSZ):
                    tc = g * GSZ + j
                    pr.v(lambda e: e.tensor_copy(kv_bf[:, :], kv_sb[:, :]),
                         reads=["kvsb", "kv_bf"], writes=["kv_bf"], tag="kvcast")
                    if l == 0 and tc == 0 and "kvb" in debug:
                        pr.dma("gpsimd", lambda e: e.dma_start(out=dbg_d["kvb"], in_=kv_bf[:, :]),
                               reads=["kv_bf"], writes=["dbgk"], tag="dbg")
                    for h in range(H):
                        c, rr = h // 2, (h % 2) * DHD
                        ab = (h % 2) * 2 + (h // 8)
                        acol = ((h // 2) % 4) * P
                        pr.t(lambda e, c=c, rr=rr, j=j, ab=ab, acol=acol: e.matmul(
                            pb[ab][:, acol:acol + P],
                            k_g[rr:rr + DHD, c * 512 + j * P:c * 512 + (j + 1) * P],
                            q_g[rr:rr + DHD, c * 512 + j * P:c * 512 + (j + 1) * P],
                            start=True, stop=True, skip_group_check=True),
                            reads=["k_g", "q_g"], writes=[f"pb{ab}"], tag="Amm")
                    attv = att[:, :].rearrange("p (hs i two q) -> p hs two i q",
                                               hs=2, i=4, two=2)
                    for ab in range(4):
                        parity, hs = ab // 2, ab % 2
                        outap = attv[:, hs, parity, :, :]
                        pr.v(lambda e, ab=ab, outap=outap: e.tensor_tensor(
                            outap, pb[ab][:, :].rearrange("p (i q) -> p i q", q=P),
                            mask4[:, :].rearrange("p (i q) -> p i q", q=P), ALU.mult),
                            reads=[f"pb{ab}", "mask4", f"att{ab}"], writes=[f"att{ab}"], tag="Aev")
                    for h in range(H):
                        c, rr = h // 2, (h % 2) * DHD
                        nb_ = 4 + (h // 8)
                        ncol = (h % 8) * DHD
                        akey = f"att{(h % 2) * 2 + (h // 8)}"
                        pr.t(lambda e, h=h, nb_=nb_, ncol=ncol, tc=tc: e.matmul(
                            pb[nb_][:, ncol:ncol + DHD], att[:, h * P:(h + 1) * P],
                            A32[:, tc * DM + h * DHD:tc * DM + (h + 1) * DHD],
                            start=True, stop=False, skip_group_check=True),
                            reads=[akey, f"v{tc}"], writes=[f"pb{nb_}"], tag="intra")
                        pr.t(lambda e, c=c, rr=rr, j=j, nb_=nb_, ncol=ncol: e.matmul(
                            pb[nb_][:, ncol:ncol + DHD],
                            q_g[rr:rr + DHD, c * 512 + j * P:c * 512 + (j + 1) * P],
                            kv_bf[rr:rr + DHD, c * DHD:(c + 1) * DHD],
                            start=False, stop=True, skip_group_check=True),
                            reads=["q_g", "kv_bf"], writes=[f"pb{nb_}"], tag="inter")
                        pr.t(lambda e, h=h, akey=akey: e.matmul(
                            pb[6][:, h:h + 1], att[:, h * P:(h + 1) * P], onesb[:, :],
                            start=True, stop=False, skip_group_check=True),
                            reads=[akey, "onesb"], writes=["pb6"], tag="den1")
                        pr.t(lambda e, c=c, rr=rr, j=j, h=h: e.matmul(
                            pb[6][:, h:h + 1],
                            q_g[rr:rr + DHD, c * 512 + j * P:c * 512 + (j + 1) * P],
                            kv_bf[rr:rr + DHD, 512 + c:513 + c],
                            start=False, stop=True, skip_group_check=True),
                            reads=["q_g", "kv_bf"], writes=["pb6"], tag="den2")
                    if l == 0 and tc == 0 and "att0" in debug:
                        pr.dma("gpsimd", lambda e: e.dma_start(out=dbg_d["att0"], in_=att[:, :]),
                               reads=["att0", "att1", "att2", "att3"], writes=["dbgk"], tag="dbg")
                    pr.s(lambda e: e.activation(den[:, :], pb[6][:, 0:H], AF.Copy, bias=ATT_EPS),
                         reads=["pb6", "den"], writes=["den"], tag="deneps")
                    if l == 0 and tc == 0 and "den0" in debug:
                        pr.dma("sync", lambda e: e.dma_start(out=dbg_d["den0"], in_=den[:, :]),
                               reads=["den"], writes=["dbgk"], tag="dbg")
                    pr.v(lambda e: e.reciprocal(rden[:, :], den[:, :]),
                         reads=["den", "rden"], writes=["rden"], tag="denrec")
                    for h in range(H):
                        nb_ = 4 + (h // 8)
                        ncol = (h % 8) * DHD
                        pr.v(lambda e, h=h, nb_=nb_, ncol=ncol: e.tensor_scalar(
                            a_tok[:, h * DHD:(h + 1) * DHD], pb[nb_][:, ncol:ncol + DHD],
                            rden[:, h:h + 1], None, ALU.mult),
                            reads=[f"pb{nb_}", "rden", "a_tok"], writes=["a_tok"], tag="adiv")
                    if l == 0 and "ao1" in debug:
                        pr.dma("gpsimd", lambda e, tc=tc: e.dma_start(
                            out=dbg_d["ao1"][tc * P:(tc + 1) * P, :], in_=a_tok[:, :]),
                            reads=["a_tok"], writes=["dbgk"], tag="dbg")
                    transpose8(a_tok[:, :], ["a_tok"], aT[:, :], "aT", bank=7, tag="trA")
                    for dm8 in range(8):
                        bi = dm8 % 2
                        wap = wslice(l, OFF_WO + dm8 * SZ_PROJ // 8, SZ_PROJ // 8, DM)
                        pr.dma("sync", lambda e, wap=wap, bi=bi: e.dma_start(
                            out=wob[bi][:, :], in_=wap),
                            reads=[lkey, f"wob{bi}"], writes=[f"wob{bi}"], tag="ldwo")
                        for half in range(2):
                            pr.t(lambda e, dm8=dm8, half=half, bi=bi: e.matmul(
                                pb[half][:, :], aT[:, dm8 * P:(dm8 + 1) * P],
                                wob[bi][:, half * 512:(half + 1) * 512],
                                start=(dm8 == 0), stop=(dm8 == 7)),
                                reads=[f"wob{bi}", "aT"], writes=[f"pb{half}"], tag="woMM")
                    for half in range(2):
                        pr.v(lambda e, half=half: e.tensor_tensor(
                            scrw[:, half * 512:(half + 1) * 512],
                            pb[half][:, :], bc[:, DM + half * 512: DM + (half + 1) * 512], ALU.add),
                            reads=[f"pb{half}", "bc1", "scrw"], writes=["scrw"], tag="woev")
                    pr.v(lambda e, tc=tc: e.tensor_tensor(st(tc), st(tc), scrw[:, :], ALU.add),
                         reads=["scrw", f"st{tc}"], writes=[f"st{tc}"], tag="resid")
                    ln_tile(tc, 2, 3, "bc2", "bc3", tag="ln1")
                    transpose8(lambda dkx, j=j: k_g[:, dkx * 512 + j * P: dkx * 512 + (j + 1) * P],
                               ["k_g"], ktokb[:, :], "ktok", bank=7, tag="trkB")
                    for h in range(H):
                        c, rr = h // 2, (h % 2) * DHD
                        pr.t(lambda e, h=h, c=c, rr=rr, tc=tc: e.matmul(
                            pb[6][rr:rr + DHD, c * DHD:(c + 1) * DHD],
                            ktokb[:, c * P + rr:c * P + rr + DHD],
                            A32[:, tc * DM + h * DHD:tc * DM + (h + 1) * DHD],
                            start=(c == 0), stop=(h == H - 1),
                            skip_group_check=True),
                            reads=["ktok", f"v{tc}"], writes=["pb6"], tag="kvchunk")
                    pr.v(lambda e: e.tensor_tensor(kv_sb[:, 0:512], kv_sb[:, 0:512], pb[6][:, :], ALU.add),
                         reads=["pb6", "kvsb"], writes=["kvsb"], tag="kvacc")
                    for c in range(8):
                        pr.v(lambda e, c=c, j=j: e.reduce_sum(
                            ztmp[:, :], k_g[:, c * 512 + j * P:c * 512 + (j + 1) * P], axis=AX.X),
                            reads=["k_g", "ztmp"], writes=["ztmp"], tag="zred2")
                        pr.v(lambda e, c=c: e.tensor_tensor(
                            kv_sb[:, 512 + c:513 + c], kv_sb[:, 512 + c:513 + c], ztmp[:, :], ALU.add),
                            reads=["ztmp", "kvsb"], writes=["kvsb"], tag="zacc2")

            if l == 0 and "pb1" in debug:
                for tc in range(NCH):
                    pr.dma("sync", lambda e, tc=tc: e.dma_start(
                        out=dbg_d["pb1"][tc * P:(tc + 1) * P, :], in_=st(tc)),
                        reads=[f"st{tc}"], writes=["dbgk"], tag="dbg")
            bcast_fill(0, rows_d[l, 2:3, :], "bc0", tag="b2")
            bcast_fill(2, rows_d[l, 5:6, :], "bc2", tag="g2")
            bcast_fill(3, rows_d[l, 6:7, :], "bc3", tag="be2")
            pr.fence(["hh"] + VKEYS, tag="fh")
            for g in range(NGT):
                if g == 0:
                    fmT_group(0, banks=(7, 6, 7, 6))
                for fc in range(FCN):
                    bi = fc % 2
                    bk_ = 6 + bi
                    wap = wslice(l, OFF_W1 + fc * DM * P, DM * P, DM)
                    pr.dma("sync", lambda e, wap=wap, bi=bi: e.dma_start(
                        out=w1b[bi][:, :], in_=wap),
                        reads=[lkey, f"w1b{bi}"], writes=[f"w1b{bi}"], tag="ldw1")
                    for dkx in range(8):
                        pr.t(lambda e, fc=fc, dkx=dkx, bi=bi, bk_=bk_: e.matmul(
                            pb[bk_][:, :], w1b[bi][:, dkx * P:(dkx + 1) * P],
                            fmT[:, dkx * 512:(dkx + 1) * 512],
                            start=(dkx == 0), stop=(dkx == 7)),
                            reads=[f"w1b{bi}", "fmT"], writes=[f"pb{bk_}"], tag="w1MM")
                    pr.s(lambda e, fc=fc, bk_=bk_: e.activation(
                        A32[:, fc * 512:(fc + 1) * 512], pb[bk_][:, :], AF.Gelu,
                        bias=b1c[:, fc:fc + 1]),
                        reads=[f"pb{bk_}", "b1c", "hh"], writes=["hh"], tag="gelu")
                for fc in range(FCN):
                    bi = fc % 2
                    wap = wslice(l, OFF_W2 + fc * DM * P, DM * P, DM)
                    pr.dma("sync", lambda e, wap=wap, bi=bi: e.dma_start(
                        out=w2b[bi][:, :], in_=wap),
                        reads=[lkey, f"w2b{bi}"], writes=[f"w2b{bi}"], tag="ldw2")
                    for j in range(GSZ):
                        for half in range(2):
                            bk_ = 2 * j + half
                            pr.t(lambda e, fc=fc, j=j, half=half, bi=bi, bk_=bk_: e.matmul(
                                pb[bk_][:, :], A32[:, fc * 512 + j * P: fc * 512 + (j + 1) * P],
                                w2b[bi][:, half * 512:(half + 1) * 512],
                                start=(fc == 0), stop=(fc == FCN - 1)),
                                reads=[f"w2b{bi}", "hh"], writes=[f"pb{bk_}"], tag="w2MM")
                for j in (3, 2, 1, 0):
                    tc = g * GSZ + j
                    for half in range(2):
                        bk_ = 2 * j + half
                        pr.v(lambda e, half=half, bk_=bk_: e.tensor_tensor(
                            scrw[:, half * 512:(half + 1) * 512], pb[bk_][:, :],
                            bc[:, half * 512:(half + 1) * 512], ALU.add),
                            reads=[f"pb{bk_}", "bc0", "scrw"], writes=["scrw"], tag="y2ev")
                    pr.v(lambda e, tc=tc: e.tensor_tensor(st(tc), st(tc), scrw[:, :], ALU.add),
                         reads=["scrw", f"st{tc}"], writes=[f"st{tc}"], tag="resid2")
                    if g + 1 < NGT:
                        if j == 3:
                            fm_chunk((g + 1) * GSZ + 0, 0, 7)
                            fm_chunk((g + 1) * GSZ + 1, 1, 6)
                        elif j == 2:
                            fm_chunk((g + 1) * GSZ + 2, 2, 5)
                        elif j == 1:
                            fm_chunk((g + 1) * GSZ + 3, 3, 3)
                for j in range(GSZ):
                    tc = g * GSZ + j
                    ln_tile(tc, 2, 3, "bc2", "bc3", tag="ln2")
            if f"l{l + 1}" in debug:
                for tc in range(NCH):
                    pr.dma("sync", lambda e, tc=tc, l=l: e.dma_start(
                        out=dbg_d[f"l{l + 1}"][tc * P:(tc + 1) * P, :], in_=st(tc)),
                        reads=[f"st{tc}"], writes=["dbgk"], tag="dbg")
            if stop_after == f"layer{l + 1}":
                trunc_finish()
                block = ctx.enter_context(nc.Block())
                pr.run(ctx, block)
                return nc

        pr.dma("sync", lambda e: e.dma_start(
            out=ztmp[:, :], in_=z_rs.rearrange("a b -> b a")),
            reads=["z_rs", "ztmp"], writes=["ztmp"], tag="ldzrs")
        pr.v(lambda e: e.reciprocal(nm[:, :], ztmp[:, :]), reads=["ztmp", "nm"], writes=["nm"], tag="rz")
        pr.dma("sync", lambda e: e.dma_start(out=slst[:, :], in_=sl_rs),
               reads=["sl_rs", "slst"], writes=["slst"], tag="ldslrs")
        pr.v(lambda e: e.tensor_scalar(xhb[:, :], slst[:, :], nm[:, 0:1], None, ALU.mult),
             reads=["slst", "nm", "xhb"], writes=["xhb"], tag="sldiv")
        pr.fence(["ktok", "slT"], tag="fslt")
        transpose8(xhb[:, :], ["xhb"], ktokb[:, :], "slT", bank=7, tag="trsl")
        for e_ in range(E_LOC):
            pr.dma("gpsimd", lambda e, e_=e_: e.dma_start(
                out=bc[:, 0:2048], in_=bg_d[e_:e_ + 1, :].broadcast_to([P, 2048])),
                writes=["bc0", "bc1"], tag="bg")
            pr.dma("gpsimd", lambda e, e_=e_: e.dma_start(
                out=bc[:, 2048:4096], in_=bu_d[e_:e_ + 1, :].broadcast_to([P, 2048])),
                writes=["bc2", "bc3"], tag="bu")
            for dkx in range(8):
                pr.dma("sync", lambda e, e_=e_, dkx=dkx: e.dma_start(
                    out=wvb[:, 0:2048], in_=wgbf[e_, dkx]),
                    reads=["wgbf", "wvb"], writes=["wvb"], tag="ldwg")
                for ft in range(4):
                    pr.t(lambda e, dkx=dkx, ft=ft, e_=e_: e.matmul(
                        pb[ft][0:DHD, :], ktokb[:, dkx * P + e_ * DHD: dkx * P + (e_ + 1) * DHD],
                        wvb[:, ft * 512:(ft + 1) * 512],
                        start=(dkx == 0), stop=(dkx == 7)),
                        reads=["slT", "wvb"], writes=[f"pb{ft}"], tag="gateMM")
                pr.dma("sync", lambda e, e_=e_, dkx=dkx: e.dma_start(
                    out=wvb[:, 2048:4096], in_=wubf[e_, dkx]),
                    reads=["wubf", "wvb"], writes=["wvb"], tag="ldwu")
                for ft in range(4):
                    pr.t(lambda e, dkx=dkx, ft=ft, e_=e_: e.matmul(
                        pb[4 + ft][0:DHD, :], ktokb[:, dkx * P + e_ * DHD: dkx * P + (e_ + 1) * DHD],
                        wvb[:, 2048 + ft * 512: 2048 + (ft + 1) * 512],
                        start=(dkx == 0), stop=(dkx == 7)),
                        reads=["slT", "wvb"], writes=[f"pb{4 + ft}"], tag="upMM")
            for ft in range(4):
                pr.v(lambda e, ft=ft: e.tensor_tensor(
                    scrw[0:DHD, 0:512], pb[ft][0:DHD, :], bc[0:DHD, ft * 512:(ft + 1) * 512], ALU.add),
                    reads=[f"pb{ft}", "bc0", "scrw"], writes=["scrw"], tag="gadd")
                pr.s(lambda e, ft=ft: e.activation(
                    gub[0:DHD, ft * 512:(ft + 1) * 512], scrw[0:DHD, 0:512], AF.Gelu),
                    reads=["scrw", "gub"], writes=["gub"], tag="ggelu")
                pr.v(lambda e, ft=ft: e.tensor_tensor(
                    upb[0:DHD, ft * 512:(ft + 1) * 512], pb[4 + ft][0:DHD, :],
                    bc[0:DHD, 2048 + ft * 512: 2048 + (ft + 1) * 512], ALU.add),
                    reads=[f"pb{4 + ft}", "bc2", "upb"], writes=["upb"], tag="uadd")
            pr.v(lambda e: e.tensor_tensor(gub[0:DHD, :], gub[0:DHD, :], upb[0:DHD, :], ALU.mult),
                 reads=["gub", "upb"], writes=["gub"], tag="gu")
            for fc in range(16):
                pr.t(lambda e, fc=fc: e.transpose(
                    pbh[7][:, fc * DHD:(fc + 1) * DHD], gub[0:DHD, fc * P:(fc + 1) * P],
                    ident[0:DHD, 0:DHD]),
                    reads=["gub", "ident"], writes=["pb7"], tag="trgu")
            pr.v(lambda e: e.tensor_copy(aT[:, :], pbh[7][:, :]),
                 reads=["pb7", "aT"], writes=["aT"], tag="trguev")
            bcast_fill(0, bd_d[e_:e_ + 1, :], "bc0", tag="bd")
            for fc in range(16):
                bi = fc % 2
                pr.dma("sync", lambda e, e_=e_, fc=fc, bi=bi: e.dma_start(
                    out=w1b[bi][:, :], in_=wdbf[e_, fc]),
                    reads=["wdbf", f"w1b{bi}"], writes=[f"w1b{bi}"], tag="ldwd")
                for half in range(2):
                    pr.t(lambda e, fc=fc, half=half, bi=bi: e.matmul(
                        pb[4 + half][0:DHD, :], aT[:, fc * DHD:(fc + 1) * DHD],
                        w1b[bi][:, half * 512:(half + 1) * 512],
                        start=(fc == 0), stop=(fc == 15)),
                        reads=[f"w1b{bi}", "aT"], writes=[f"pb{4 + half}"], tag="eoMM")
            for half in range(2):
                pr.v(lambda e, half=half: e.tensor_tensor(
                    xhb[0:DHD, half * 512:(half + 1) * 512], pb[4 + half][0:DHD, :],
                    bc[0:DHD, half * 512:(half + 1) * 512], ALU.add),
                    reads=[f"pb{4 + half}", "bc0", "xhb"], writes=["xhb"], tag="eoev")
            pr.dma("gpsimd", lambda e, e_=e_: e.dma_start(
                out=eo_in[e_ * DHD:(e_ + 1) * DHD, :], in_=xhb[0:DHD, :]),
                reads=["xhb"], writes=["eo_in"], tag="steo")
        pr.coll(lambda e: e.collective_compute(
            "AllGather", ALU.bypass, replica_groups=[list(range(NCORES))],
            ins=[eo_in.opt()], outs=[eo_out.opt()]),
            reads=["eo_in"], writes=["eo_out"], tag="ag_eo")

        pr.fence(["hh", "ct"] + VKEYS, tag="fct")
        for c in range(8):
            pr.dma("sync", lambda e, c=c: e.dma_start(
                out=A32[:, c * 2048:(c + 1) * 2048][:, 0:T], in_=ct_dr[c]),
                reads=["ct_dr", "ct"], writes=["ct"], tag="ldct")
        pr.fence(["q_g", "k_g", "eout"], tag="feo")
        for c in range(8):
            pr.dma("sync", lambda e, c=c: e.dma_start(
                out=B16[:, c * 1024:(c + 1) * 1024], in_=eo_out[c * P:(c + 1) * P, :]),
                reads=["eo_out", "eout"], writes=["eout"], tag="ldeo")
        bcast_fill(2, orows_d[0:1, :], "bc2", tag="gf")
        bcast_fill(3, orows_d[1:2, :], "bc3", tag="bf")
        x0b = wvb.bitcast(F32)[:, 0:DM]
        o_sb = wvb.bitcast(F32)[:, DM:2 * DM]
        pr.fence(["x0b", "o_sb", "wvb"], tag="fx0")
        for tc in range(NCH):
            for half in range(2):
                for c in range(8):
                    pr.t(lambda e, tc=tc, half=half, c=c: e.matmul(
                        pb[half][:, :], A32[:, c * 2048 + tc * P: c * 2048 + (tc + 1) * P],
                        B16[:, c * 1024 + half * 512: c * 1024 + (half + 1) * 512],
                        start=(c == 0), stop=(c == 7)),
                        reads=["ct", "eout"], writes=[f"pb{half}"], tag="mixMM")
            for half in range(2):
                pr.s(lambda e, half=half: e.activation(
                    scrw[:, half * 512:(half + 1) * 512], pb[half][:, :], AF.Silu),
                    reads=[f"pb{half}", "scrw"], writes=["scrw"], tag="silu")
            ln_tile(tc, 2, 3, "bc2", "bc3", tag="lnf")
            pr.dma("sync", lambda e, tc=tc: e.dma_start(out=x0b, in_=x_d[tc * P:(tc + 1) * P, :]),
                   reads=[], writes=["x0b"], tag="ldx0")
            pr.v(lambda e, tc=tc: e.tensor_tensor(o_sb, st(tc), scrw[:, :], ALU.mult),
                 reads=[f"st{tc}", "scrw", "o_sb"], writes=["o_sb"], tag="sg_mul")
            pr.v(lambda e: e.tensor_tensor(o_sb, o_sb, x0b, ALU.add),
                 reads=["o_sb", "x0b"], writes=["o_sb"], tag="o_add")
            pr.dma("sync", lambda e, tc=tc: e.dma_start(out=out_d[tc * P:(tc + 1) * P, :], in_=o_sb),
                   reads=["o_sb"], writes=["outd"], tag="stout")
        pr.op("sync", lambda e: e.nop(nofuse=True), reads=["outd", "dbgk"], tag="final")

        block = ctx.enter_context(nc.Block())
        pr.run(ctx, block)
    return nc



def _pack_layer(Wq, Wk, Wv, Wo, W1, W2):
    out = np.empty(LAYER_ELEMS, np.float32)
    o = 0

    def put(a):
        nonlocal o
        f = a.reshape(-1)
        out[o:o + f.size] = f
        o += f.size
    for c in range(8):
        put(Wq[:, c * P:(c + 1) * P].reshape(8, P, P).transpose(1, 0, 2))
    for c in range(8):
        put(Wk[:, c * P:(c + 1) * P].reshape(8, P, P).transpose(1, 0, 2))
    for h in range(2):
        put(Wv[:, h * 512:(h + 1) * 512].reshape(8, P, 512).transpose(1, 0, 2))
    for dm8 in range(8):
        put(Wo[dm8 * P:(dm8 + 1) * P, :])
    for fc in range(FCN):
        put(W1[:, fc * P:(fc + 1) * P].reshape(8, P, P).transpose(1, 0, 2))
    for fc in range(FCN):
        put(W2[fc * P:(fc + 1) * P, :])
    assert o == LAYER_ELEMS
    return out


def make_core_inputs(x_full, w, T=2048, NL=6):
    Ttot = x_full.shape[0]
    assert Ttot == NCORES * T
    packed = np.stack([_pack_layer(w["Wq"][l], w["Wk"][l], w["Wv"][l],
                                   w["Wo"][l], w["W1"][l], w["W2"][l])
                       for l in range(NL)])
    shards = packed.reshape(NL, NCORES, P, SHARD_COLS)
    bq = np.stack([w["bq"][l].reshape(8, P).T for l in range(NL)])
    bk = np.stack([w["bk"][l].reshape(8, P).T for l in range(NL)])
    b1c_in = np.stack([w["b1"][l].reshape(32, P).T for l in range(NL)])
    rows = np.stack([np.stack([w["bv"][l], w["bo"][l], w["b2"][l],
                               w["g1"][l], w["be1"][l], w["g2"][l], w["be2"][l]])
                     for l in range(NL)])
    orows = np.zeros((8, DM), np.float32)
    orows[0] = w["gf"]
    orows[1] = w["bf"]
    orows[2] = w["norm_g"]
    orows[3] = w["norm_b"]
    orows[4] = w["slot_g"]
    orows[5] = w["slot_b"]
    sed = w["slot_embeds"].reshape(ES, DM)
    maskU = np.triu(np.ones((P, P), np.float32))
    identm = np.eye(P, dtype=np.float32)
    ins = []
    for r in range(NCORES):
        rmask = np.zeros((P, NCORES), np.float32)
        rmask[:, :r] = 1.0
        e0 = E_LOC * r
        ins.append(dict(
            x=np.ascontiguousarray(x_full[r * T:(r + 1) * T]),
            wsh=np.ascontiguousarray(shards[:, r]),
            bq=bq, bk=bk, b1cx=b1c_in, rows=rows, orows=orows,
            scalex=w["scale"].reshape(1, 1),
            sed=sed,
            wgx=np.ascontiguousarray(w["Wg"][e0:e0 + E_LOC].reshape(E_LOC, 8, P, 2048)),
            wux=np.ascontiguousarray(w["Wu"][e0:e0 + E_LOC].reshape(E_LOC, 8, P, 2048)),
            wdx=np.ascontiguousarray(w["Wd"][e0:e0 + E_LOC].reshape(E_LOC, 16, P, DM)),
            bgx=np.ascontiguousarray(w["bg"][e0:e0 + E_LOC]),
            bux=np.ascontiguousarray(w["bu"][e0:e0 + E_LOC]),
            bdx=np.ascontiguousarray(w["bd"][e0:e0 + E_LOC]),
            maskU=maskU, identx=identm, rmaskx=rmask,
        ))
    return ins


def kernel(L0_attn_x, L0_attn_Wq, L0_attn_bq, L0_attn_Wk, L0_attn_bk,
           L0_attn_Wv, L0_attn_bv, L0_attn_Wo, L0_attn_bo, L0_attn_g1,
           L0_attn_be1, L0_attn_W1, L0_attn_b1, L0_attn_W2, L0_attn_b2,
           L0_attn_g2, L0_attn_be2, L0_attn_gf, L0_attn_bf, L0_ffn_norm_g,
           L0_ffn_norm_b, L0_ffn_slot_g, L0_ffn_slot_b, L0_ffn_scale,
           L0_ffn_slot_embeds, L0_ffn_Wg, L0_ffn_bg, L0_ffn_Wu, L0_ffn_bu,
           L0_ffn_Wd, L0_ffn_bd):
    f = lambda a: np.asarray(a, dtype=np.float32)
    T, NL = 2048, 6
    x = f(L0_attn_x)[0]
    xu = np.ascontiguousarray(x[::-1])
    w = dict(Wq=f(L0_attn_Wq), bq=f(L0_attn_bq), Wk=f(L0_attn_Wk), bk=f(L0_attn_bk),
             Wv=f(L0_attn_Wv), bv=f(L0_attn_bv), Wo=f(L0_attn_Wo), bo=f(L0_attn_bo),
             g1=f(L0_attn_g1), be1=f(L0_attn_be1), W1=f(L0_attn_W1), b1=f(L0_attn_b1),
             W2=f(L0_attn_W2), b2=f(L0_attn_b2), g2=f(L0_attn_g2), be2=f(L0_attn_be2),
             gf=f(L0_attn_gf), bf=f(L0_attn_bf), norm_g=f(L0_ffn_norm_g),
             norm_b=f(L0_ffn_norm_b), slot_g=f(L0_ffn_slot_g), slot_b=f(L0_ffn_slot_b),
             scale=f(L0_ffn_scale), slot_embeds=f(L0_ffn_slot_embeds),
             Wg=f(L0_ffn_Wg), bg=f(L0_ffn_bg), Wu=f(L0_ffn_Wu), bu=f(L0_ffn_bu),
             Wd=f(L0_ffn_Wd), bd=f(L0_ffn_bd))
    ins = make_core_inputs(xu, w, T=T, NL=NL)
    nc = build_program(T=T, NL=NL)
    res = run_bass_kernel_spmd(nc, ins, core_ids=list(range(NCORES)))
    out_u = np.concatenate([np.asarray(res.results[r]["out"]) for r in range(NCORES)], axis=0)
    return np.ascontiguousarray(out_u[::-1])[None, :, :]


# version 08 (final)
# speedup vs baseline: 1.1338x; 1.1338x over previous
import numpy as np
import ml_dtypes
import concourse.bass as bass
import concourse.mybir as mybir
from concourse.bass_utils import run_bass_kernel_spmd
from contextlib import ExitStack

F32 = mybir.dt.float32
BF16 = mybir.dt.bfloat16
AF = mybir.ActivationFunctionType
ALU = mybir.AluOpType
AX = mybir.AxisListType

NCORES = 8
P = 128
DM = 1024
H = 16
DHD = 64
FF = 4096
FCN = 32
LN_EPS = 1e-5
ATT_EPS = 1e-6
E_LOC = 2
ES = 1024

SZ_PROJ = DM * DM
OFF_WQ = 0
OFF_WK = SZ_PROJ
OFF_WV = 2 * SZ_PROJ
OFF_WO = 3 * SZ_PROJ
OFF_W1 = 4 * SZ_PROJ
OFF_W2 = OFF_W1 + DM * FF
LAYER_ELEMS = OFF_W2 + DM * FF
SHARD_COLS = LAYER_ELEMS // NCORES // P

ENGINES = ("tensor", "vector", "scalar", "gpsimd", "sync")
SEM_ROTATE = 24000
NDMA_SEMS = 10


class _Op:
    __slots__ = ("eng", "emit", "reads", "writes", "dma", "coll", "idx",
                 "deps", "inc", "sem", "val", "tag", "pre_wait", "fence_f")


class Prog:
    def __init__(self, nc):
        self.nc = nc
        self.ops = []
        self.wr = {}
        self.rd = {}

    def op(self, eng, emit, reads=(), writes=(), dma=False, coll=False, tag="",
           fence_f=False):
        o = _Op()
        o.eng, o.emit, o.dma, o.coll, o.tag = eng, emit, dma, coll, tag
        o.fence_f = fence_f
        o.reads, o.writes = tuple(reads), tuple(writes)
        o.idx = len(self.ops)
        deps = set()
        for k in o.reads:
            if k in self.wr:
                deps.add(self.wr[k])
        for k in o.writes:
            if k in self.wr:
                deps.add(self.wr[k])
            for r in self.rd.get(k, ()):
                deps.add(r)
        deps.discard(o.idx)
        o.deps = deps
        for k in o.reads:
            self.rd.setdefault(k, []).append(o.idx)
        for k in o.writes:
            self.wr[k] = o.idx
            self.rd[k] = []
        self.ops.append(o)
        return o.idx

    def t(self, emit, reads=(), writes=(), tag=""):
        return self.op("tensor", emit, reads, writes, tag=tag)

    def v(self, emit, reads=(), writes=(), tag=""):
        return self.op("vector", emit, reads, writes, tag=tag)

    def s(self, emit, reads=(), writes=(), tag=""):
        return self.op("scalar", emit, reads, writes, tag=tag)

    def g(self, emit, reads=(), writes=(), tag=""):
        return self.op("gpsimd", emit, reads, writes, tag=tag)

    def dma(self, queue, emit, reads=(), writes=(), tag=""):
        return self.op(queue, emit, reads, writes, dma=True, tag=tag)

    def coll(self, emit, reads=(), writes=(), tag=""):
        return self.op("gpsimd", emit, reads, writes, coll=True, tag=tag)

    def fence(self, keys, reads=(), tag="fence"):
        return self.op("vector", lambda e: e.nop(nofuse=True), reads, keys, tag=tag,
                       fence_f=True)

    def finalize(self, ctx):
        nc = self.nc
        ops = self.ops
        for o in ops:
            pruned = set()
            for d in o.deps:
                od = ops[d]
                if (od.eng == o.eng == "tensor" and not od.dma and not od.coll
                        and not o.dma and not o.coll):
                    continue
                pruned.add(d)
            same = [d for d in pruned
                    if ops[d].eng == o.eng and not ops[d].dma and not ops[d].coll]
            if len(same) > 1:
                keep = max(same)
                for d in same:
                    if d != keep:
                        pruned.discard(d)
            o.deps = pruned
        need = set()
        for o in ops:
            need |= o.deps
        counters = {e: 0 for e in ENGINES}
        sem_pool = {e: [] for e in ENGINES}
        sem_base = {e: [] for e in ENGINES}

        def cur_sem(eng):
            if not sem_pool[eng] or counters[eng] - sem_base[eng][-1] > SEM_ROTATE:
                h = ctx.enter_context(nc.semaphore(f"s_{eng}_{len(sem_pool[eng])}"))
                sem_pool[eng].append(h)
                sem_base[eng].append(counters[eng])
            return sem_pool[eng][-1], sem_base[eng][-1]

        dma_pool = {}
        dma_rr = {}
        for o in ops:
            o.pre_wait = None
            if o.idx in need or o.coll or o.dma:
                if o.coll or o.fence_f:
                    h = ctx.enter_context(nc.semaphore(f"s_coll_{o.idx}"))
                    o.sem, o.inc, o.val = h, 1, 1
                elif o.dma:
                    pool = dma_pool.setdefault(o.eng, [])
                    if len(pool) < NDMA_SEMS:
                        h = ctx.enter_context(nc.semaphore(f"sd_{o.eng}_{len(pool)}"))
                        pool.append([h, 0])
                        slot = pool[-1]
                    else:
                        i = dma_rr.get(o.eng, 0)
                        slot = pool[i % NDMA_SEMS]
                        dma_rr[o.eng] = i + 1
                    if slot[1] > 0:
                        o.pre_wait = (slot[0], 16 * slot[1])
                    slot[1] += 1
                    o.sem, o.inc, o.val = slot[0], 16, 16 * slot[1]
                else:
                    o.inc = 1
                    h, base = cur_sem(o.eng)
                    counters[o.eng] += 1
                    o.sem, o.val = h, counters[o.eng] - base
            else:
                o.sem = None

    def emit_all(self, block):
        ops = self.ops
        for eng_name in ENGINES:
            mine = [o for o in ops if o.eng == eng_name]
            if not mine:
                continue

            def body(eng, mine=mine):
                watermark = {}
                for o in mine:
                    if o.pre_wait is not None:
                        sem, val = o.pre_wait
                        if watermark.get(id(sem), -1) < val:
                            eng.wait_ge(sem, val)
                            watermark[id(sem)] = val
                    waits = {}
                    for d in o.deps:
                        od = ops[d]
                        if od.sem is None:
                            continue
                        sid = id(od.sem)
                        if waits.get(sid, (None, -1))[1] < od.val:
                            waits[sid] = (od.sem, od.val)
                    for sem, val in waits.values():
                        sid = id(sem)
                        if watermark.get(sid, -1) < val:
                            eng.wait_ge(sem, val)
                            watermark[sid] = val
                    inst = o.emit(eng)
                    if o.sem is not None:
                        if inst is None:
                            raise RuntimeError(f"op {o.idx} {o.tag}: needs inc, emit->None")
                        inst.then_inc(o.sem, o.inc)
            getattr(block, eng_name)(body)

    def run(self, ctx, block):
        self.finalize(ctx)
        self.emit_all(block)



def build_program(T=2048, NL=6, debug=(), stop_after=None, skips=()):
    NCH = T // P
    GSZ = 4
    NGT = NCH // GSZ
    VKEYS = [f"v{tc}" for tc in range(NCH)]
    nc = bass.Bass("TRN2", target_bir_lowering=False, debug=False)

    x_d = nc.dram_tensor("x", [T, DM], F32, kind="ExternalInput").ap()
    wsh_d = nc.dram_tensor("wsh", [NL, P, SHARD_COLS], F32, kind="ExternalInput").ap()
    bq_d = nc.dram_tensor("bq", [NL, P, 8], F32, kind="ExternalInput").ap()
    bk_d = nc.dram_tensor("bk", [NL, P, 8], F32, kind="ExternalInput").ap()
    b1c_d = nc.dram_tensor("b1cx", [NL, P, 32], F32, kind="ExternalInput").ap()
    rows_d = nc.dram_tensor("rows", [NL, 7, DM], F32, kind="ExternalInput").ap()
    orows_d = nc.dram_tensor("orows", [8, DM], F32, kind="ExternalInput").ap()
    scale_d = nc.dram_tensor("scalex", [1, 1], F32, kind="ExternalInput").ap()
    sed_d = nc.dram_tensor("sed", [ES, DM], F32, kind="ExternalInput").ap()
    wg_d = nc.dram_tensor("wgx", [E_LOC, 8, P, 2048], F32, kind="ExternalInput").ap()
    wu_d = nc.dram_tensor("wux", [E_LOC, 8, P, 2048], F32, kind="ExternalInput").ap()
    wd_d = nc.dram_tensor("wdx", [E_LOC, 16, P, DM], F32, kind="ExternalInput").ap()
    bg_d = nc.dram_tensor("bgx", [E_LOC, 2048], F32, kind="ExternalInput").ap()
    bu_d = nc.dram_tensor("bux", [E_LOC, 2048], F32, kind="ExternalInput").ap()
    bd_d = nc.dram_tensor("bdx", [E_LOC, DM], F32, kind="ExternalInput").ap()
    maskU_d = nc.dram_tensor("maskU", [P, P], F32, kind="ExternalInput").ap()
    ident_d = nc.dram_tensor("identx", [P, P], F32, kind="ExternalInput").ap()
    rmask_d = nc.dram_tensor("rmaskx", [P, NCORES], F32, kind="ExternalInput").ap()
    out_d = nc.dram_tensor("out", [T, DM], F32, kind="ExternalOutput").ap()
    dbg_d = {}
    _DBG_SHAPES = {"qg": [P, 4096], "kg": [P, 4096], "vv": [P, DM],
                   "att0": [P, 2048], "den0": [P, H], "fmt": [P, 4096],
                   "kvb": [P, 520], "wqb": [P, DM]}
    for tagn in debug:
        dbg_d[tagn] = nc.dram_tensor(f"dbg_{tagn}", _DBG_SHAPES.get(tagn, [T, DM]),
                                     F32, kind="ExternalOutput").ap()

    wbf_sh = [nc.dram_tensor(f"wbf_sh{l}", [P, SHARD_COLS], BF16).ap() for l in range(NL)]
    wgth = [nc.dram_tensor(f"wgth{l}", [NCORES * P, SHARD_COLS], BF16).ap() for l in range(NL)]
    kvt = [nc.dram_tensor(f"kvt{l}", [P, 520], F32).ap() for l in range(NL)]
    kvg = [nc.dram_tensor(f"kvg{l}", [NCORES * P, 520], F32).ap() for l in range(NL)]
    mx_in = nc.dram_tensor("mx_in", [8, P], F32).ap()
    mx_out = nc.dram_tensor("mx_out", [8, P], F32).ap()
    z_in = nc.dram_tensor("z_in", [NCORES, P], F32).ap()
    z_rs = nc.dram_tensor("z_rs", [1, P], F32).ap()
    sl_in = nc.dram_tensor("sl_in", [ES, DM], F32).ap()
    sl_rs = nc.dram_tensor("sl_rs", [P, DM], F32).ap()
    disp_dr = nc.dram_tensor("disp_dr", [T, ES], BF16).ap()
    xh_dr = nc.dram_tensor("xh_dr", [T, DM], BF16).ap()
    ct_dr = nc.dram_tensor("ct_dr", [8, P, T], BF16).ap()
    eo_in = nc.dram_tensor("eo_in", [P, DM], BF16).ap()
    eo_out = nc.dram_tensor("eo_out", [ES, DM], BF16).ap()
    wgbf = nc.dram_tensor("wgbf", [E_LOC, 8, P, 2048], BF16).ap()
    wubf = nc.dram_tensor("wubf", [E_LOC, 8, P, 2048], BF16).ap()
    wdbf = nc.dram_tensor("wdbf", [E_LOC, 16, P, DM], BF16).ap()

    def wslice(l, off, n, cols):
        flat = wgth[l].rearrange("a b -> (a b)")
        return flat[off:off + n].rearrange("(p q) -> p q", q=cols)

    with ExitStack() as ctx:
        sbt = lambda n, s, d: ctx.enter_context(nc.sbuf_tensor(n, s, d))
        pst = lambda n, s, d: ctx.enter_context(nc.psum_tensor(n, s, d))
        pr = Prog(nc)

        stream = sbt("stream", [P, NCH * DM], F32)
        A32 = sbt("A32", [P, 16384], BF16)
        B16 = sbt("B16", [P, 8192], BF16)
        fmT = sbt("fmT", [P, 4096], BF16)
        att = sbt("att", [P, 2048], BF16)
        a_tok = sbt("a_tok", [P, DM], BF16)
        aT = sbt("aT", [P, DM], BF16)
        ktokb = sbt("ktokb", [P, DM], BF16)
        kv_sb = sbt("kv_sb", [P, 520], F32)
        kv_bf = sbt("kv_bf", [P, 520], BF16)
        tots = sbt("tots", [P, 520], F32)
        gtile = sbt("gtile", [P, 520], F32)
        wqb = [sbt(f"wqb{i}", [P, DM], BF16) for i in range(2)]
        wkb = [sbt(f"wkb{i}", [P, DM], BF16) for i in range(2)]
        wvb = sbt("wvb", [P, 4096], BF16)
        wob = [sbt(f"wob{i}", [P, DM], BF16) for i in range(2)]
        w1b = [sbt(f"w1b{i}", [P, DM], BF16) for i in range(2)]
        w2b = [sbt(f"w2b{i}", [P, DM], BF16) for i in range(2)]
        bc = sbt("bc", [P, 4096], BF16)
        bqc = sbt("bqc", [P, 8], F32)
        bkc = sbt("bkc", [P, 8], F32)
        b1c = sbt("b1c", [P, 32], F32)
        scr1 = sbt("scr1", [P, 512], F32)
        scr2 = sbt("scr2", [P, 512], F32)
        scrw = sbt("scrw", [P, DM], F32)
        bnst = sbt("bnst", [P, 12], F32)
        mv = sbt("mv", [P, 2], F32)
        nm = sbt("nm", [P, 1], F32)
        rstd = sbt("rstd", [P, 1], F32)
        den = sbt("den", [P, H], F32)
        rden = sbt("rden", [P, H], F32)
        ztmp = sbt("ztmp", [P, 1], F32)
        maskU = sbt("maskUs", [P, P], BF16)
        mask4 = sbt("mask4", [P, 512], BF16)
        ident = sbt("idents", [P, P], BF16)
        onesb = sbt("onesb", [P, 1], BF16)
        rmask = sbt("rmasks", [P, NCORES], F32)
        zcol = sbt("zcol", [P, 8], F32)
        dspb = sbt("dspb", [P, ES], BF16)
        xhb = sbt("xhb", [P, DM], BF16)
        slst = sbt("slst", [P, DM], F32)
        rowmax = sbt("rowmax", [P, 1], F32)
        rowsum = sbt("rowsum", [P, 1], F32)
        gub = sbt("gub", [P, 2048], BF16)
        upb = sbt("upb", [P, 2048], F32)
        scl = sbt("scl", [P, 1], F32)
        ceps = sbt("ceps", [P, 1], F32)

        pb = [pst(f"pb{i}", [P, 512], F32) for i in range(8)]
        pbh = [b.bitcast(BF16) for b in pb]

        st = lambda tc: stream[:, tc * DM:(tc + 1) * DM]

        def bcast_fill(slot, src_row_ap, key, n=DM, tag="bc"):
            pr.dma("gpsimd", lambda e: e.dma_start(
                out=bc[:, slot * DM: slot * DM + n],
                in_=src_row_ap.broadcast_to([P, n])),
                writes=[key], tag=tag)

        def rsqrt_of(src_ap, rkey):
            pr.s(lambda e: e.activation(rstd[:, :], src_ap, AF.Ln, bias=ceps[:, 0:1]),
                 reads=[rkey, "rstd", "ceps"], writes=["rstd"], tag="lnv")
            pr.s(lambda e: e.activation(rstd[:, :], rstd[:, :], AF.Exp, scale=-0.5),
                 reads=["rstd"], writes=["rstd"], tag="expv")

        def ln_stats(src_ap, rkeys):
            pr.v(lambda e: e.bn_stats(bnst[:, 0:6], src_ap[:, 0:512]),
                 reads=list(rkeys), writes=["bnst"], tag="bns")
            pr.v(lambda e: e.bn_stats(bnst[:, 6:12], src_ap[:, 512:1024]),
                 reads=list(rkeys) + ["bnst"], writes=["bnst"], tag="bns")
            pr.v(lambda e: e.bn_aggr(mv[:, :], bnst[:, :]),
                 reads=["bnst"], writes=["mv"], tag="bna")
            pr.v(lambda e: e.tensor_scalar(nm[:, :], mv[:, 0:1], -1.0, None, ALU.mult),
                 reads=["mv"], writes=["nm"], tag="nm")
            rsqrt_of(mv[:, 1:2], "mv")

        def ln_tile(tc, gslot, bslot, gkey, bkey, tag="ln"):
            s_ = st(tc)
            k = f"st{tc}"
            ln_stats(s_, [k])
            pr.v(lambda e: e.tensor_scalar(s_, s_, nm[:, 0:1], rstd[:, 0:1], ALU.add, ALU.mult),
                 reads=[k, "nm", "rstd"], writes=[k], tag=tag)
            if gslot is not None:
                pr.v(lambda e: e.tensor_tensor(s_, s_, bc[:, gslot * DM:(gslot + 1) * DM], ALU.mult),
                     reads=[k, gkey], writes=[k], tag=tag)
                pr.v(lambda e: e.tensor_tensor(s_, s_, bc[:, bslot * DM:(bslot + 1) * DM], ALU.add),
                     reads=[k, bkey], writes=[k], tag=tag)

        def transpose8(src, rkeys, dst_ap, wkey, bank=7, tag="tr", ev_in=None,
                       evac_eng="v"):
            for dkx in range(8):
                sap = src(dkx) if callable(src) else src[:, dkx * P:(dkx + 1) * P]
                pr.t(lambda e, sap=sap, dkx=dkx: e.transpose(
                    pbh[bank][:, dkx * P:(dkx + 1) * P], sap, ident[:, :]),
                    reads=list(rkeys) + ["ident"], writes=[f"pb{bank}"], tag=tag)
            inap = ev_in if ev_in is not None else pbh[bank][:, :]
            if evac_eng == "s":
                pr.s(lambda e: e.activation(dst_ap, inap, AF.Copy, bias=0.0),
                     reads=[f"pb{bank}"], writes=[wkey], tag=tag + "ev")
            else:
                pr.v(lambda e: e.tensor_copy(dst_ap, inap),
                     reads=[f"pb{bank}"], writes=[wkey], tag=tag + "ev")

        def fm_evac(bank, bias_ap, out_ap, rkeys, wkeys, tag="fm"):
            pr.s(lambda e: e.activation(scr1[:, :], pb[bank][:, :], AF.Relu, bias=bias_ap),
                 reads=list(rkeys) + ["scr1"], writes=["scr1"], tag=tag)
            pr.v(lambda e: e.tensor_scalar(scr2[:, :], pb[bank][:, :], bias_ap, 0.0, ALU.add, ALU.min),
                 reads=list(rkeys) + ["scr2", "scr1"], writes=["scr2"], tag=tag)
            pr.s(lambda e: e.activation(scr2[:, :], scr2[:, :], AF.Exp),
                 reads=["scr2"], writes=["scr2"], tag=tag)
            pr.v(lambda e: e.tensor_tensor(out_ap, scr1[:, :], scr2[:, :], ALU.add),
                 reads=["scr1", "scr2"], writes=list(wkeys), tag=tag)

        pr.dma("gpsimd", lambda e: e.dma_start(out=maskU[:, :], in_=maskU_d), writes=["maskU"])
        pr.dma("gpsimd", lambda e: e.dma_start(out=ident[:, :], in_=ident_d), writes=["ident"])
        pr.dma("sync", lambda e: e.dma_start(out=rmask[:, :], in_=rmask_d), writes=["rmask"])
        for i in range(4):
            pr.v(lambda e, i=i: e.tensor_copy(mask4[:, i * P:(i + 1) * P], maskU[:, :]),
                 reads=["maskU", "mask4"], writes=["mask4"])
        pr.v(lambda e: e.memset(onesb[:, :], 1.0), writes=["onesb"])
        pr.v(lambda e: e.memset(ceps[:, :], LN_EPS), writes=["ceps"])

        def emit_gather(l):
            pr.coll(lambda e, l=l: e.collective_compute(
                "AllGather", ALU.bypass, replica_groups=[list(range(NCORES))],
                ins=[wbf_sh[l].opt()], outs=[wgth[l].opt()]),
                reads=[f"wbf{l}"], writes=[f"wg{l}"], tag=f"gath{l}")

        for l in range(NL):
            pr.dma("gpsimd", lambda e, l=l: e.dma_start(
                out=wbf_sh[l].rearrange("p (a q) -> p a q", q=2048),
                in_=wsh_d[l].rearrange("p (a q) -> p a q", q=2048)),
                writes=[f"wbf{l}"], tag=f"cast{l}")
        for l in range(min(2, NL)):
            emit_gather(l)

        for tc in range(NCH):
            pr.dma("sync", lambda e, tc=tc: e.dma_start(out=st(tc), in_=x_d[tc * P:(tc + 1) * P, :]),
                   writes=[f"st{tc}"], tag="ldx")
        for tc in range(NCH):
            ln_tile(tc, None, None, None, None, tag="lnxn")
        if "xn" in debug:
            for tc in range(NCH):
                pr.dma("sync", lambda e, tc=tc: e.dma_start(
                    out=dbg_d["xn"][tc * P:(tc + 1) * P, :], in_=st(tc)),
                    reads=[f"st{tc}"], writes=["dbgk"], tag="dbg")

        bcast_fill(0, orows_d[4:5, :], "bc0", tag="sg")
        bcast_fill(1, orows_d[5:6, :], "bc1", tag="sb")
        pr.dma("gpsimd", lambda e: e.dma_start(out=scl[:, :], in_=scale_d.broadcast_to([P, 1])),
               writes=["scl"], tag="scl")
        for c in range(8):
            pr.dma("sync", lambda e, c=c: e.dma_start(out=scrw[:, :], in_=sed_d[c * P:(c + 1) * P, :]),
                   reads=["scrw"], writes=["scrw"], tag="ldse")
            ln_stats(scrw[:, :], ["scrw"])
            pr.v(lambda e: e.tensor_scalar(scrw[:, :], scrw[:, :], nm[:, 0:1], rstd[:, 0:1], ALU.add, ALU.mult),
                 reads=["scrw", "nm", "rstd"], writes=["scrw"], tag="selin")
            pr.v(lambda e: e.tensor_tensor(scrw[:, :], scrw[:, :], bc[:, 0:DM], ALU.mult),
                 reads=["scrw", "bc0"], writes=["scrw"], tag="seg")
            pr.v(lambda e: e.tensor_tensor(scrw[:, :], scrw[:, :], bc[:, DM:2 * DM], ALU.add),
                 reads=["scrw", "bc1"], writes=["scrw"], tag="seb")
            pr.v(lambda e: e.tensor_scalar(xhb[:, :], scrw[:, :], scl[:, 0:1], None, ALU.mult),
                 reads=["scrw", "scl", "xhb"], writes=["xhb"], tag="sescale")
            transpose8(xhb[:, :], ["xhb"],
                       B16[:, :].rearrange("p (dk q) -> p dk q", q=1024)[:, :, c * P:(c + 1) * P],
                       "seT", bank=7, tag="trse",
                       ev_in=pbh[7][:, :].rearrange("p (dk q) -> p dk q", q=P))

        pr.v(lambda e: e.memset(zcol[:, :], -3.0e38), writes=["zcol"], tag="gmax0")
        bcast_fill(2, orows_d[2:3, :], "bc2", tag="ng")
        bcast_fill(3, orows_d[3:4, :], "bc3", tag="nb")

        def make_xhat(tc, store):
            s_ = st(tc)
            k = f"st{tc}"
            ln_stats(s_, [k])
            pr.v(lambda e: e.tensor_scalar(scrw[:, :], s_, nm[:, 0:1], rstd[:, 0:1], ALU.add, ALU.mult),
                 reads=[k, "nm", "rstd", "scrw"], writes=["scrw"], tag="xh1")
            pr.v(lambda e: e.tensor_tensor(scrw[:, :], scrw[:, :], bc[:, 2 * DM:3 * DM], ALU.mult),
                 reads=["scrw", "bc2"], writes=["scrw"], tag="xh2")
            pr.v(lambda e: e.tensor_tensor(xhb[:, :], scrw[:, :], bc[:, 3 * DM:4 * DM], ALU.add),
                 reads=["scrw", "bc3", "xhb"], writes=["xhb"], tag="xh3")
            if store:
                pr.dma("gpsimd", lambda e, tc=tc: e.dma_start(
                    out=xh_dr[tc * P:(tc + 1) * P, :], in_=xhb[:, :]),
                    reads=["xhb"], writes=["xh_dr"], tag="st_xh")

        def logits_psum(tc):
            transpose8(xhb[:, :], ["xhb"], a_tok[:, :], "a_tok", bank=6, tag="trxh")
            for half in range(2):
                for dkx in range(8):
                    pr.t(lambda e, half=half, dkx=dkx: e.matmul(
                        pb[half][:, :], a_tok[:, dkx * P:(dkx + 1) * P],
                        B16[:, dkx * 1024 + half * 512: dkx * 1024 + (half + 1) * 512],
                        start=(dkx == 0), stop=(dkx == 7)),
                        reads=["a_tok", "seT"], writes=[f"pb{half}"], tag="mmlog")

        for tc in range(NCH):
            make_xhat(tc, store=True)
            logits_psum(tc)
            for half in range(2):
                pr.v(lambda e, half=half: e.tensor_copy(
                    dspb[:, half * 512:(half + 1) * 512], pb[half][:, :]),
                    reads=[f"pb{half}", "dspb"], writes=["dspb"], tag="logbf")
            for c in range(8):
                sl7 = c % 8
                pr.t(lambda e, c=c, sl7=sl7: e.transpose(
                    pbh[7][:, sl7 * P:(sl7 + 1) * P], dspb[:, c * P:(c + 1) * P], ident[:, :]),
                    reads=["dspb", "ident"], writes=["pb7"], tag="trlog")
                pr.v(lambda e, sl7=sl7: e.reduce_max(ztmp[:, :], pbh[7][:, sl7 * P:(sl7 + 1) * P], axis=AX.X),
                     reads=["pb7", "ztmp"], writes=["ztmp"], tag="redmax")
                pr.v(lambda e, c=c: e.tensor_tensor(zcol[:, c:c + 1], zcol[:, c:c + 1], ztmp[:, :], ALU.max),
                     reads=["ztmp", "zcol"], writes=["zcol"], tag="accmax")
        for c in range(8):
            pr.dma("gpsimd", lambda e, c=c: e.dma_start(
                out=mx_in[c:c + 1, :].rearrange("a b -> (a b)").rearrange("(b a) -> b a", a=1),
                in_=zcol[:, c:c + 1]),
                reads=["zcol"], writes=["mx_in"], tag="stmax")
        pr.coll(lambda e: e.collective_compute(
            "AllReduce", ALU.max, replica_groups=[list(range(NCORES))],
            ins=[mx_in.opt()], outs=[mx_out.opt()]),
            reads=["mx_in"], writes=["mx_out"], tag="ar_max")
        gmax_row = mx_out.rearrange("a b -> (a b)").rearrange("(o n) -> o n", o=1)
        pr.dma("gpsimd", lambda e: e.dma_start(out=bc[:, 0:DM], in_=gmax_row.broadcast_to([P, ES])),
               reads=["mx_out", "bc0"], writes=["bc0"], tag="bcmax")

        for tc in range(NCH):
            make_xhat(tc, store=False)
            logits_psum(tc)
            pr.v(lambda e: e.reduce_max(rowmax[:, :], pb[0][:, :], axis=AX.X),
                 reads=["pb0", "rowmax"], writes=["rowmax"], tag="cmax")
            pr.v(lambda e: e.reduce_max(ztmp[:, :], pb[1][:, :], axis=AX.X),
                 reads=["pb1", "ztmp"], writes=["ztmp"], tag="cmax")
            pr.v(lambda e: e.tensor_tensor(rowmax[:, :], rowmax[:, :], ztmp[:, :], ALU.max),
                 reads=["rowmax", "ztmp"], writes=["rowmax"], tag="cmax")
            pr.v(lambda e: e.tensor_scalar(rowmax[:, :], rowmax[:, :], -1.0, None, ALU.mult),
                 reads=["rowmax"], writes=["rowmax"], tag="cneg")
            for half in range(2):
                pr.s(lambda e, half=half: e.activation(
                    scrw[:, half * 512:(half + 1) * 512], pb[half][:, :], AF.Exp, bias=rowmax[:, 0:1]),
                    reads=[f"pb{half}", "rowmax", "scrw"], writes=["scrw"], tag="cexp")
            pr.v(lambda e: e.reduce_sum(rowsum[:, :], scrw[:, :], axis=AX.X),
                 reads=["scrw"], writes=["rowsum"], tag="csum")
            pr.v(lambda e: e.reciprocal(rowsum[:, :], rowsum[:, :]),
                 reads=["rowsum"], writes=["rowsum"], tag="crec")
            pr.v(lambda e: e.tensor_scalar(xhb[:, :], scrw[:, :], rowsum[:, 0:1], None, ALU.mult),
                 reads=["scrw", "rowsum", "xhb"], writes=["xhb"], tag="cdiv")
            transpose8(xhb[:, :], ["xhb"], a_tok[:, :], "a_tok", bank=6, tag="trcb")
            for c in range(8):
                pr.dma("sync", lambda e, c=c, tc=tc: e.dma_start(
                    out=ct_dr[c, :, tc * P:(tc + 1) * P], in_=a_tok[:, c * P:(c + 1) * P]),
                    reads=["a_tok"], writes=["ct_dr"], tag="stct")
            for half in range(2):
                pr.v(lambda e, half=half: e.tensor_tensor(
                    scrw[:, half * 512:(half + 1) * 512], pb[half][:, :],
                    bc[:, half * 512:(half + 1) * 512], ALU.subtract),
                    reads=[f"pb{half}", "bc0", "scrw"], writes=["scrw"], tag="dsub")
            for half in range(2):
                pr.s(lambda e, half=half: e.activation(
                    dspb[:, half * 512:(half + 1) * 512], scrw[:, half * 512:(half + 1) * 512], AF.Exp),
                    reads=["scrw", "dspb"], writes=["dspb"], tag="dexp")
            pr.dma("gpsimd", lambda e, tc=tc: e.dma_start(
                out=disp_dr[tc * P:(tc + 1) * P, :], in_=dspb[:, :]),
                reads=["dspb"], writes=["disp_dr"], tag="stdsp")
            for half in range(2):
                pr.t(lambda e, half=half, tc=tc: e.matmul(
                    pb[4 + half][0:1, :], onesb[:, :], dspb[:, half * 512:(half + 1) * 512],
                    start=(tc == 0), stop=(tc == NCH - 1)),
                    reads=["dspb", "onesb"], writes=[f"pb{4 + half}"], tag="zmm")
        pr.v(lambda e: e.tensor_copy(slst[0:1, 0:512], pb[4][0:1, :]),
             reads=["pb4", "slst"], writes=["slst"], tag="zev")
        pr.v(lambda e: e.tensor_copy(slst[0:1, 512:1024], pb[5][0:1, :]),
             reads=["pb5", "slst"], writes=["slst"], tag="zev")
        pr.dma("gpsimd", lambda e: e.dma_start(
            out=z_in.rearrange("a b -> (a b)").rearrange("(o n) -> o n", o=1), in_=slst[0:1, 0:ES]),
            reads=["slst"], writes=["z_in"], tag="stz")
        pr.coll(lambda e: e.collective_compute(
            "ReduceScatter", ALU.add, replica_groups=[list(range(NCORES))],
            ins=[z_in.opt()], outs=[z_rs.opt()]),
            reads=["z_in"], writes=["z_rs"], tag="rs_z")

        for half8 in range(2):
            for tc in range(NCH):
                pr.dma("sync", lambda e, tc=tc: e.dma_start(
                    out=dspb[:, :], in_=disp_dr[tc * P:(tc + 1) * P, :]),
                    reads=["disp_dr", "dspb"], writes=["dspb"], tag="lddsp")
                pr.dma("sync", lambda e, tc=tc: e.dma_start(
                    out=xhb[:, :], in_=xh_dr[tc * P:(tc + 1) * P, :]),
                    reads=["xh_dr", "xhb"], writes=["xhb"], tag="ldxh")
                for ce in range(4):
                    c = half8 * 4 + ce
                    for half in range(2):
                        pr.t(lambda e, c=c, ce=ce, half=half, tc=tc: e.matmul(
                            pb[2 * ce + half][:, :], dspb[:, c * P:(c + 1) * P],
                            xhb[:, half * 512:(half + 1) * 512],
                            start=(tc == 0), stop=(tc == NCH - 1)),
                            reads=["dspb", "xhb"], writes=[f"pb{2 * ce + half}"], tag="slmm")
            for ce in range(4):
                c = half8 * 4 + ce
                for half in range(2):
                    pr.v(lambda e, ce=ce, half=half: e.tensor_copy(
                        slst[:, half * 512:(half + 1) * 512], pb[2 * ce + half][:, :]),
                        reads=[f"pb{2 * ce + half}", "slst"], writes=["slst"], tag="slev")
                pr.dma("gpsimd", lambda e, c=c: e.dma_start(
                    out=sl_in[c * P:(c + 1) * P, :], in_=slst[:, :]),
                    reads=["slst"], writes=["sl_in"], tag="stsl")
        pr.coll(lambda e: e.collective_compute(
            "ReduceScatter", ALU.add, replica_groups=[list(range(NCORES))],
            ins=[sl_in.opt()], outs=[sl_rs.opt()]),
            reads=["sl_in"], writes=["sl_rs"], tag="rs_sl")

        for l in range(2, NL):
            emit_gather(l)

        pr.dma("gpsimd", lambda e: e.dma_start(out=wgbf, in_=wg_d),
               writes=["wgbf"], tag="cast_wg")
        pr.dma("gpsimd", lambda e: e.dma_start(out=wubf, in_=wu_d),
               writes=["wubf"], tag="cast_wu")
        pr.dma("gpsimd", lambda e: e.dma_start(out=wdbf, in_=wd_d),
               writes=["wdbf"], tag="cast_wd")

        def trunc_finish():
            for tc in range(NCH):
                pr.dma("sync", lambda e, tc=tc: e.dma_start(
                    out=out_d[tc * P:(tc + 1) * P, :], in_=st(tc)),
                    reads=[f"st{tc}"], writes=["outd"], tag="stout")
            pr.op("sync", lambda e: e.nop(nofuse=True), reads=["outd", "dbgk"], tag="final")

        if stop_after == "outer1":
            trunc_finish()
            block = ctx.enter_context(nc.Block())
            pr.run(ctx, block)
            return nc

        k_g = B16[:, 4096:8192]
        q_g = B16[:, 0:4096]

        def fm_chunk(tc, j, bank, evac_eng="v"):
            pr.s(lambda e, tc=tc: e.activation(a_tok[:, :], st(tc), AF.Copy, bias=0.0),
                 reads=[f"st{tc}", "a_tok"], writes=["a_tok"], tag="castst")
            transpose8(a_tok[:, :], ["a_tok"],
                       fmT[:, :].rearrange("p (dk q) -> p dk q", q=512)[:, :, j * P:(j + 1) * P],
                       "fmT", bank=bank, tag="trx",
                       ev_in=pbh[bank][:, :].rearrange("p (dk q) -> p dk q", q=P),
                       evac_eng=evac_eng)

        def fmT_group(g, banks=(7, 7, 7, 7)):
            for j in range(GSZ):
                fm_chunk(g * GSZ + j, j, banks[j])

        for l in range(NL):
            lkey = f"wg{l}"
            pr.dma("sync", lambda e, l=l: e.dma_start(out=bqc[:, :], in_=bq_d[l]),
                   reads=["bqc"], writes=["bqc"], tag="ldbq")
            pr.dma("sync", lambda e, l=l: e.dma_start(out=bkc[:, :], in_=bk_d[l]),
                   reads=["bkc"], writes=["bkc"], tag="ldbk")
            pr.dma("sync", lambda e, l=l: e.dma_start(out=b1c[:, :], in_=b1c_d[l]),
                   reads=["b1c"], writes=["b1c"], tag="ldb1")

            def proj_feat(woff, wbufs, bias_cols, bkey, out_sb, outkeys, banks, tagp):
                for c8 in range(8):
                    bi = c8 % 2
                    bk_ = banks[bi]
                    wap = wslice(l, woff + c8 * SZ_PROJ // 8, SZ_PROJ // 8, DM)
                    pr.dma("sync", lambda e, wap=wap, bi=bi: e.dma_start(
                        out=wbufs[bi][:, :], in_=wap),
                        reads=[lkey, f"{tagp}b{bi}"], writes=[f"{tagp}b{bi}"], tag=f"ld{tagp}")
                    if "kmm" in skips:
                        pr.v(lambda e, bi=bi: e.tensor_copy(
                            out_sb[:, c8 * 512:(c8 + 1) * 512], wbufs[bi][:, 0:512]),
                            reads=[f"{tagp}b{bi}"], writes=outkeys, tag="skipev")
                        continue
                    for dkx in range(8):
                        pr.t(lambda e, c8=c8, dkx=dkx, bi=bi, bk_=bk_: e.matmul(
                            pb[bk_][:, :], wbufs[bi][:, dkx * P:(dkx + 1) * P],
                            fmT[:, dkx * 512:(dkx + 1) * 512],
                            start=(dkx == 0), stop=(dkx == 7)),
                            reads=[f"{tagp}b{bi}", "fmT"], writes=[f"pb{bk_}"], tag=f"mm{tagp}")
                    if "fmev" in skips:
                        pr.v(lambda e, bk_=bk_, c8=c8: e.tensor_copy(
                            out_sb[:, c8 * 512:(c8 + 1) * 512], pb[bk_][:, :]),
                            reads=[f"pb{bk_}"], writes=outkeys, tag="plainev")
                    else:
                        fm_evac(bk_, bias_cols[:, c8:c8 + 1],
                                out_sb[:, c8 * 512:(c8 + 1) * 512],
                                [f"pb{bk_}", bkey], outkeys, tag=f"fm{tagp}")

            bcast_fill(0, rows_d[l, 0:1, :], "bc0", tag="bv")
            pr.fence(VKEYS + ["hh"], tag="fv")
            for g in range(NGT):
                fmT_group(g)
                if "kproj" not in skips:
                    proj_feat(OFF_WK, wkb, bkc, "bkc", k_g, ["k_g"], (0, 1), "wk")
                for half in range(2 if "v" not in skips else 0):
                    wap = wslice(l, OFF_WV + half * SZ_PROJ // 2, SZ_PROJ // 2, 4096)
                    pr.dma("sync", lambda e, wap=wap: e.dma_start(
                        out=wvb[:, :], in_=wap),
                        reads=[lkey, "wvb"], writes=["wvb"], tag="ldwv")
                    for j in range(GSZ):
                        tc = g * GSZ + j
                        bk_ = 2 + (j % 2)
                        for dkx in range(8):
                            pr.t(lambda e, j=j, dkx=dkx, bk_=bk_: e.matmul(
                                pb[bk_][:, :],
                                fmT[:, dkx * 512 + j * P: dkx * 512 + (j + 1) * P],
                                wvb[:, dkx * 512:(dkx + 1) * 512],
                                start=(dkx == 0), stop=(dkx == 7)),
                                reads=["wvb", "fmT"], writes=[f"pb{bk_}"], tag="vmm")
                        pr.v(lambda e, tc=tc, half=half, bk_=bk_: e.tensor_tensor(
                            A32[:, tc * DM + half * 512: tc * DM + (half + 1) * 512],
                            pb[bk_][:, :], bc[:, half * 512:(half + 1) * 512], ALU.add),
                            reads=[f"pb{bk_}", "bc0", f"v{tc}"], writes=[f"v{tc}"], tag="vev")
                for j in range(GSZ if "kvtot" not in skips else 0):
                    tc = g * GSZ + j
                    transpose8(lambda dkx, j=j: k_g[:, dkx * 512 + j * P: dkx * 512 + (j + 1) * P],
                               ["k_g"], ktokb[:, :], "ktok", bank=7, tag="trkA")
                    for h in range(H):
                        c, rr = h // 2, (h % 2) * DHD
                        pr.t(lambda e, h=h, c=c, rr=rr, tc=tc: e.matmul(
                            pb[6][rr:rr + DHD, c * DHD:(c + 1) * DHD],
                            ktokb[:, c * P + rr: c * P + rr + DHD],
                            A32[:, tc * DM + h * DHD: tc * DM + (h + 1) * DHD],
                            start=(tc == 0 and c == 0), stop=(tc == NCH - 1 and h == H - 1),
                            skip_group_check=True),
                            reads=["ktok", f"v{tc}"], writes=["pb6"], tag="kvtot")
                    for c in range(8):
                        pr.v(lambda e, c=c, j=j: e.reduce_sum(
                            ztmp[:, :], k_g[:, c * 512 + j * P: c * 512 + (j + 1) * P], axis=AX.X),
                            reads=["k_g", "ztmp"], writes=["ztmp"], tag="zred")
                        if tc == 0:
                            pr.v(lambda e, c=c: e.tensor_copy(tots[:, 512 + c:513 + c], ztmp[:, :]),
                                 reads=["ztmp", "tots"], writes=["tots"], tag="zinit")
                        else:
                            pr.v(lambda e, c=c: e.tensor_tensor(
                                tots[:, 512 + c:513 + c], tots[:, 512 + c:513 + c], ztmp[:, :], ALU.add),
                                reads=["ztmp", "tots"], writes=["tots"], tag="zacc")
            if "kvtot" in skips:
                pr.v(lambda e: e.memset(tots[:, :], 0.0), reads=["tots"], writes=["tots"], tag="kvz")
            else:
                pr.v(lambda e: e.tensor_copy(tots[:, 0:512], pb[6][:, :]),
                     reads=["pb6", "tots"], writes=["tots"], tag="kvev")
            pr.dma("gpsimd", lambda e, l=l: e.dma_start(out=kvt[l], in_=tots[:, :]),
                   reads=["tots"], writes=[f"kvt{l}"], tag="stkv")
            if "kvcoll" not in skips:
                pr.coll(lambda e, l=l: e.collective_compute(
                    "AllGather", ALU.bypass, replica_groups=[list(range(NCORES))],
                    ins=[kvt[l].opt()], outs=[kvg[l].opt()]),
                    reads=[f"kvt{l}"], writes=[f"kvgd{l}"], tag=f"ag_kv{l}")
            pr.v(lambda e: e.memset(kv_sb[:, :], 0.0),
                 reads=["kv_bf"], writes=["kvsb"], tag="kv0")
            for r in range(NCORES if "kvcoll" not in skips else 0):
                pr.dma("sync", lambda e, r=r, l=l: e.dma_start(
                    out=gtile[:, :], in_=kvg[l][r * P:(r + 1) * P, :]),
                    reads=[f"kvgd{l}", "gtile"], writes=["gtile"], tag="ldkvg")
                pr.v(lambda e, r=r: e.scalar_tensor_tensor(
                    kv_sb[:, :], gtile[:, :], rmask[:, r:r + 1], kv_sb[:, :], ALU.mult, ALU.add),
                    reads=["gtile", "rmask", "kvsb"], writes=["kvsb"], tag="pref")

            if stop_after == f"passA{l + 1}":
                trunc_finish()
                block = ctx.enter_context(nc.Block())
                pr.run(ctx, block)
                return nc
            bcast_fill(1, rows_d[l, 1:2, :], "bc1", tag="bo")
            bcast_fill(2, rows_d[l, 3:4, :], "bc2", tag="g1")
            bcast_fill(3, rows_d[l, 4:5, :], "bc3", tag="be1")
            for g in range(NGT):
                fmT_group(g)
                proj_feat(OFF_WK, wkb, bkc, "bkc", k_g, ["k_g"], (0, 1), "wk")
                proj_feat(OFF_WQ, wqb, bqc, "bqc", q_g, ["q_g"], (2, 3), "wq")
                if l == 0 and g == 0 and "wqb" in debug:
                    pr.dma("gpsimd", lambda e: e.dma_start(out=dbg_d["wqb"], in_=wqb[0][:, :]),
                           reads=["wqb0"], writes=["dbgk"], tag="dbg")
                if l == 0 and g == 0:
                    for nm_, buf_, key_ in (("qg", q_g, "q_g"), ("kg", k_g, "k_g"),
                                            ("fmt", fmT[:, :], "fmT")):
                        if nm_ in debug:
                            pr.dma("gpsimd", lambda e, nm_=nm_, buf_=buf_: e.dma_start(
                                out=dbg_d[nm_], in_=buf_),
                                reads=[key_], writes=["dbgk"], tag="dbg")
                    if "vv" in debug:
                        pr.dma("gpsimd", lambda e: e.dma_start(
                            out=dbg_d["vv"], in_=A32[:, 0:DM]),
                            reads=["v0"], writes=["dbgk"], tag="dbg")
                for j in range(GSZ):
                    tc = g * GSZ + j
                    pr.v(lambda e: e.tensor_copy(kv_bf[:, :], kv_sb[:, :]),
                         reads=["kvsb", "kv_bf"], writes=["kv_bf"], tag="kvcast")
                    if l == 0 and tc == 0 and "kvb" in debug:
                        pr.dma("gpsimd", lambda e: e.dma_start(out=dbg_d["kvb"], in_=kv_bf[:, :]),
                               reads=["kv_bf"], writes=["dbgk"], tag="dbg")
                    for h in range(H):
                        c, rr = h // 2, (h % 2) * DHD
                        ab = (h % 2) * 2 + (h // 8)
                        acol = ((h // 2) % 4) * P
                        pr.t(lambda e, c=c, rr=rr, j=j, ab=ab, acol=acol: e.matmul(
                            pb[ab][:, acol:acol + P],
                            k_g[rr:rr + DHD, c * 512 + j * P:c * 512 + (j + 1) * P],
                            q_g[rr:rr + DHD, c * 512 + j * P:c * 512 + (j + 1) * P],
                            start=True, stop=True, skip_group_check=True),
                            reads=["k_g", "q_g"], writes=[f"pb{ab}"], tag="Amm")
                    attv = att[:, :].rearrange("p (hs i two q) -> p hs two i q",
                                               hs=2, i=4, two=2)
                    for ab in range(4):
                        parity, hs = ab // 2, ab % 2
                        outap = attv[:, hs, parity, :, :]
                        pr.v(lambda e, ab=ab, outap=outap: e.tensor_tensor(
                            outap, pb[ab][:, :].rearrange("p (i q) -> p i q", q=P),
                            mask4[:, :].rearrange("p (i q) -> p i q", q=P), ALU.mult),
                            reads=[f"pb{ab}", "mask4", f"att{ab}"], writes=[f"att{ab}"], tag="Aev")
                    for h in range(H):
                        c, rr = h // 2, (h % 2) * DHD
                        nb_ = 4 + (h // 8)
                        ncol = (h % 8) * DHD
                        akey = f"att{(h % 2) * 2 + (h // 8)}"
                        pr.t(lambda e, h=h, nb_=nb_, ncol=ncol, tc=tc: e.matmul(
                            pb[nb_][:, ncol:ncol + DHD], att[:, h * P:(h + 1) * P],
                            A32[:, tc * DM + h * DHD:tc * DM + (h + 1) * DHD],
                            start=True, stop=False, skip_group_check=True),
                            reads=[akey, f"v{tc}"], writes=[f"pb{nb_}"], tag="intra")
                        pr.t(lambda e, c=c, rr=rr, j=j, nb_=nb_, ncol=ncol: e.matmul(
                            pb[nb_][:, ncol:ncol + DHD],
                            q_g[rr:rr + DHD, c * 512 + j * P:c * 512 + (j + 1) * P],
                            kv_bf[rr:rr + DHD, c * DHD:(c + 1) * DHD],
                            start=False, stop=True, skip_group_check=True),
                            reads=["q_g", "kv_bf"], writes=[f"pb{nb_}"], tag="inter")
                        pr.t(lambda e, h=h, akey=akey: e.matmul(
                            pb[6][:, h:h + 1], att[:, h * P:(h + 1) * P], onesb[:, :],
                            start=True, stop=False, skip_group_check=True),
                            reads=[akey, "onesb"], writes=["pb6"], tag="den1")
                        pr.t(lambda e, c=c, rr=rr, j=j, h=h: e.matmul(
                            pb[6][:, h:h + 1],
                            q_g[rr:rr + DHD, c * 512 + j * P:c * 512 + (j + 1) * P],
                            kv_bf[rr:rr + DHD, 512 + c:513 + c],
                            start=False, stop=True, skip_group_check=True),
                            reads=["q_g", "kv_bf"], writes=["pb6"], tag="den2")
                    if l == 0 and tc == 0 and "att0" in debug:
                        pr.dma("gpsimd", lambda e: e.dma_start(out=dbg_d["att0"], in_=att[:, :]),
                               reads=["att0", "att1", "att2", "att3"], writes=["dbgk"], tag="dbg")
                    pr.s(lambda e: e.activation(den[:, :], pb[6][:, 0:H], AF.Copy, bias=ATT_EPS),
                         reads=["pb6", "den"], writes=["den"], tag="deneps")
                    if l == 0 and tc == 0 and "den0" in debug:
                        pr.dma("sync", lambda e: e.dma_start(out=dbg_d["den0"], in_=den[:, :]),
                               reads=["den"], writes=["dbgk"], tag="dbg")
                    pr.v(lambda e: e.reciprocal(rden[:, :], den[:, :]),
                         reads=["den", "rden"], writes=["rden"], tag="denrec")
                    for h in range(H):
                        nb_ = 4 + (h // 8)
                        ncol = (h % 8) * DHD
                        pr.v(lambda e, h=h, nb_=nb_, ncol=ncol: e.tensor_scalar(
                            a_tok[:, h * DHD:(h + 1) * DHD], pb[nb_][:, ncol:ncol + DHD],
                            rden[:, h:h + 1], None, ALU.mult),
                            reads=[f"pb{nb_}", "rden", "a_tok"], writes=["a_tok"], tag="adiv")
                    if l == 0 and "ao1" in debug:
                        pr.dma("gpsimd", lambda e, tc=tc: e.dma_start(
                            out=dbg_d["ao1"][tc * P:(tc + 1) * P, :], in_=a_tok[:, :]),
                            reads=["a_tok"], writes=["dbgk"], tag="dbg")
                    transpose8(a_tok[:, :], ["a_tok"], aT[:, :], "aT", bank=7, tag="trA")
                    for dm8 in range(8):
                        bi = dm8 % 2
                        wap = wslice(l, OFF_WO + dm8 * SZ_PROJ // 8, SZ_PROJ // 8, DM)
                        pr.dma("sync", lambda e, wap=wap, bi=bi: e.dma_start(
                            out=wob[bi][:, :], in_=wap),
                            reads=[lkey, f"wob{bi}"], writes=[f"wob{bi}"], tag="ldwo")
                        for half in range(2):
                            pr.t(lambda e, dm8=dm8, half=half, bi=bi: e.matmul(
                                pb[half][:, :], aT[:, dm8 * P:(dm8 + 1) * P],
                                wob[bi][:, half * 512:(half + 1) * 512],
                                start=(dm8 == 0), stop=(dm8 == 7)),
                                reads=[f"wob{bi}", "aT"], writes=[f"pb{half}"], tag="woMM")
                    for half in range(2):
                        pr.v(lambda e, half=half: e.tensor_tensor(
                            scrw[:, half * 512:(half + 1) * 512],
                            pb[half][:, :], bc[:, DM + half * 512: DM + (half + 1) * 512], ALU.add),
                            reads=[f"pb{half}", "bc1", "scrw"], writes=["scrw"], tag="woev")
                    pr.v(lambda e, tc=tc: e.tensor_tensor(st(tc), st(tc), scrw[:, :], ALU.add),
                         reads=["scrw", f"st{tc}"], writes=[f"st{tc}"], tag="resid")
                    ln_tile(tc, 2, 3, "bc2", "bc3", tag="ln1")
                    transpose8(lambda dkx, j=j: k_g[:, dkx * 512 + j * P: dkx * 512 + (j + 1) * P],
                               ["k_g"], ktokb[:, :], "ktok", bank=7, tag="trkB")
                    for h in range(H):
                        c, rr = h // 2, (h % 2) * DHD
                        pr.t(lambda e, h=h, c=c, rr=rr, tc=tc: e.matmul(
                            pb[6][rr:rr + DHD, c * DHD:(c + 1) * DHD],
                            ktokb[:, c * P + rr:c * P + rr + DHD],
                            A32[:, tc * DM + h * DHD:tc * DM + (h + 1) * DHD],
                            start=(c == 0), stop=(h == H - 1),
                            skip_group_check=True),
                            reads=["ktok", f"v{tc}"], writes=["pb6"], tag="kvchunk")
                    pr.v(lambda e: e.tensor_tensor(kv_sb[:, 0:512], kv_sb[:, 0:512], pb[6][:, :], ALU.add),
                         reads=["pb6", "kvsb"], writes=["kvsb"], tag="kvacc")
                    for c in range(8):
                        pr.v(lambda e, c=c, j=j: e.reduce_sum(
                            ztmp[:, :], k_g[:, c * 512 + j * P:c * 512 + (j + 1) * P], axis=AX.X),
                            reads=["k_g", "ztmp"], writes=["ztmp"], tag="zred2")
                        pr.v(lambda e, c=c: e.tensor_tensor(
                            kv_sb[:, 512 + c:513 + c], kv_sb[:, 512 + c:513 + c], ztmp[:, :], ALU.add),
                            reads=["ztmp", "kvsb"], writes=["kvsb"], tag="zacc2")

            if l == 0 and "pb1" in debug:
                for tc in range(NCH):
                    pr.dma("sync", lambda e, tc=tc: e.dma_start(
                        out=dbg_d["pb1"][tc * P:(tc + 1) * P, :], in_=st(tc)),
                        reads=[f"st{tc}"], writes=["dbgk"], tag="dbg")
            bcast_fill(0, rows_d[l, 2:3, :], "bc0", tag="b2")
            bcast_fill(2, rows_d[l, 5:6, :], "bc2", tag="g2")
            bcast_fill(3, rows_d[l, 6:7, :], "bc3", tag="be2")
            pr.fence(["hh"] + VKEYS, tag="fh")
            for g in range(NGT):
                if g == 0:
                    for j2 in range(GSZ):
                        fm_chunk(j2, j2, 7 - (j2 % 2), evac_eng="s")
                for fc in range(FCN):
                    bi = fc % 2
                    bk_ = 6 + bi
                    wap = wslice(l, OFF_W1 + fc * DM * P, DM * P, DM)
                    pr.dma("sync", lambda e, wap=wap, bi=bi: e.dma_start(
                        out=w1b[bi][:, :], in_=wap),
                        reads=[lkey, f"w1b{bi}"], writes=[f"w1b{bi}"], tag="ldw1")
                    for dkx in range(8):
                        pr.t(lambda e, fc=fc, dkx=dkx, bi=bi, bk_=bk_: e.matmul(
                            pb[bk_][:, :], w1b[bi][:, dkx * P:(dkx + 1) * P],
                            fmT[:, dkx * 512:(dkx + 1) * 512],
                            start=(dkx == 0), stop=(dkx == 7)),
                            reads=[f"w1b{bi}", "fmT"], writes=[f"pb{bk_}"], tag="w1MM")
                    pr.s(lambda e, fc=fc, bk_=bk_: e.activation(
                        A32[:, fc * 512:(fc + 1) * 512], pb[bk_][:, :], AF.Gelu,
                        bias=b1c[:, fc:fc + 1]),
                        reads=[f"pb{bk_}", "b1c", "hh"], writes=["hh"], tag="gelu")
                for fc in range(FCN):
                    bi = fc % 2
                    wap = wslice(l, OFF_W2 + fc * DM * P, DM * P, DM)
                    pr.dma("sync", lambda e, wap=wap, bi=bi: e.dma_start(
                        out=w2b[bi][:, :], in_=wap),
                        reads=[lkey, f"w2b{bi}"], writes=[f"w2b{bi}"], tag="ldw2")
                    for j in range(GSZ):
                        for half in range(2):
                            bk_ = 2 * j + half
                            pr.t(lambda e, fc=fc, j=j, half=half, bi=bi, bk_=bk_: e.matmul(
                                pb[bk_][:, :], A32[:, fc * 512 + j * P: fc * 512 + (j + 1) * P],
                                w2b[bi][:, half * 512:(half + 1) * 512],
                                start=(fc == 0), stop=(fc == FCN - 1)),
                                reads=[f"w2b{bi}", "hh"], writes=[f"pb{bk_}"], tag="w2MM")
                for j in (3, 2, 1, 0):
                    tc = g * GSZ + j
                    for half in range(2):
                        bk_ = 2 * j + half
                        pr.v(lambda e, half=half, bk_=bk_: e.tensor_tensor(
                            scrw[:, half * 512:(half + 1) * 512], pb[bk_][:, :],
                            bc[:, half * 512:(half + 1) * 512], ALU.add),
                            reads=[f"pb{bk_}", "bc0", "scrw"], writes=["scrw"], tag="y2ev")
                    pr.v(lambda e, tc=tc: e.tensor_tensor(st(tc), st(tc), scrw[:, :], ALU.add),
                         reads=["scrw", f"st{tc}"], writes=[f"st{tc}"], tag="resid2")
                    if g + 1 < NGT and j == 3:
                        for j2 in range(GSZ):
                            fm_chunk((g + 1) * GSZ + j2, j2, 7 - (j2 % 2),
                                     evac_eng="s")
                for j in range(GSZ):
                    tc = g * GSZ + j
                    ln_tile(tc, 2, 3, "bc2", "bc3", tag="ln2")
            if f"l{l + 1}" in debug:
                for tc in range(NCH):
                    pr.dma("sync", lambda e, tc=tc, l=l: e.dma_start(
                        out=dbg_d[f"l{l + 1}"][tc * P:(tc + 1) * P, :], in_=st(tc)),
                        reads=[f"st{tc}"], writes=["dbgk"], tag="dbg")
            if stop_after == f"layer{l + 1}":
                trunc_finish()
                block = ctx.enter_context(nc.Block())
                pr.run(ctx, block)
                return nc

        pr.dma("sync", lambda e: e.dma_start(
            out=ztmp[:, :], in_=z_rs.rearrange("a b -> b a")),
            reads=["z_rs", "ztmp"], writes=["ztmp"], tag="ldzrs")
        pr.v(lambda e: e.reciprocal(nm[:, :], ztmp[:, :]), reads=["ztmp", "nm"], writes=["nm"], tag="rz")
        pr.dma("sync", lambda e: e.dma_start(out=slst[:, :], in_=sl_rs),
               reads=["sl_rs", "slst"], writes=["slst"], tag="ldslrs")
        pr.v(lambda e: e.tensor_scalar(xhb[:, :], slst[:, :], nm[:, 0:1], None, ALU.mult),
             reads=["slst", "nm", "xhb"], writes=["xhb"], tag="sldiv")
        pr.fence(["ktok", "slT"], tag="fslt")
        transpose8(xhb[:, :], ["xhb"], ktokb[:, :], "slT", bank=7, tag="trsl")
        for e_ in range(E_LOC):
            pr.dma("gpsimd", lambda e, e_=e_: e.dma_start(
                out=bc[:, 0:2048], in_=bg_d[e_:e_ + 1, :].broadcast_to([P, 2048])),
                writes=["bc0", "bc1"], tag="bg")
            pr.dma("gpsimd", lambda e, e_=e_: e.dma_start(
                out=bc[:, 2048:4096], in_=bu_d[e_:e_ + 1, :].broadcast_to([P, 2048])),
                writes=["bc2", "bc3"], tag="bu")
            for dkx in range(8):
                pr.dma("sync", lambda e, e_=e_, dkx=dkx: e.dma_start(
                    out=wvb[:, 0:2048], in_=wgbf[e_, dkx]),
                    reads=["wgbf", "wvb"], writes=["wvb"], tag="ldwg")
                for ft in range(4):
                    pr.t(lambda e, dkx=dkx, ft=ft, e_=e_: e.matmul(
                        pb[ft][0:DHD, :], ktokb[:, dkx * P + e_ * DHD: dkx * P + (e_ + 1) * DHD],
                        wvb[:, ft * 512:(ft + 1) * 512],
                        start=(dkx == 0), stop=(dkx == 7)),
                        reads=["slT", "wvb"], writes=[f"pb{ft}"], tag="gateMM")
                pr.dma("sync", lambda e, e_=e_, dkx=dkx: e.dma_start(
                    out=wvb[:, 2048:4096], in_=wubf[e_, dkx]),
                    reads=["wubf", "wvb"], writes=["wvb"], tag="ldwu")
                for ft in range(4):
                    pr.t(lambda e, dkx=dkx, ft=ft, e_=e_: e.matmul(
                        pb[4 + ft][0:DHD, :], ktokb[:, dkx * P + e_ * DHD: dkx * P + (e_ + 1) * DHD],
                        wvb[:, 2048 + ft * 512: 2048 + (ft + 1) * 512],
                        start=(dkx == 0), stop=(dkx == 7)),
                        reads=["slT", "wvb"], writes=[f"pb{4 + ft}"], tag="upMM")
            for ft in range(4):
                pr.v(lambda e, ft=ft: e.tensor_tensor(
                    scrw[0:DHD, 0:512], pb[ft][0:DHD, :], bc[0:DHD, ft * 512:(ft + 1) * 512], ALU.add),
                    reads=[f"pb{ft}", "bc0", "scrw"], writes=["scrw"], tag="gadd")
                pr.s(lambda e, ft=ft: e.activation(
                    gub[0:DHD, ft * 512:(ft + 1) * 512], scrw[0:DHD, 0:512], AF.Gelu),
                    reads=["scrw", "gub"], writes=["gub"], tag="ggelu")
                pr.v(lambda e, ft=ft: e.tensor_tensor(
                    upb[0:DHD, ft * 512:(ft + 1) * 512], pb[4 + ft][0:DHD, :],
                    bc[0:DHD, 2048 + ft * 512: 2048 + (ft + 1) * 512], ALU.add),
                    reads=[f"pb{4 + ft}", "bc2", "upb"], writes=["upb"], tag="uadd")
            pr.v(lambda e: e.tensor_tensor(gub[0:DHD, :], gub[0:DHD, :], upb[0:DHD, :], ALU.mult),
                 reads=["gub", "upb"], writes=["gub"], tag="gu")
            for fc in range(16):
                pr.t(lambda e, fc=fc: e.transpose(
                    pbh[7][:, fc * DHD:(fc + 1) * DHD], gub[0:DHD, fc * P:(fc + 1) * P],
                    ident[0:DHD, 0:DHD]),
                    reads=["gub", "ident"], writes=["pb7"], tag="trgu")
            pr.v(lambda e: e.tensor_copy(aT[:, :], pbh[7][:, :]),
                 reads=["pb7", "aT"], writes=["aT"], tag="trguev")
            bcast_fill(0, bd_d[e_:e_ + 1, :], "bc0", tag="bd")
            for fc in range(16):
                bi = fc % 2
                pr.dma("sync", lambda e, e_=e_, fc=fc, bi=bi: e.dma_start(
                    out=w1b[bi][:, :], in_=wdbf[e_, fc]),
                    reads=["wdbf", f"w1b{bi}"], writes=[f"w1b{bi}"], tag="ldwd")
                for half in range(2):
                    pr.t(lambda e, fc=fc, half=half, bi=bi: e.matmul(
                        pb[4 + half][0:DHD, :], aT[:, fc * DHD:(fc + 1) * DHD],
                        w1b[bi][:, half * 512:(half + 1) * 512],
                        start=(fc == 0), stop=(fc == 15)),
                        reads=[f"w1b{bi}", "aT"], writes=[f"pb{4 + half}"], tag="eoMM")
            for half in range(2):
                pr.v(lambda e, half=half: e.tensor_tensor(
                    xhb[0:DHD, half * 512:(half + 1) * 512], pb[4 + half][0:DHD, :],
                    bc[0:DHD, half * 512:(half + 1) * 512], ALU.add),
                    reads=[f"pb{4 + half}", "bc0", "xhb"], writes=["xhb"], tag="eoev")
            pr.dma("gpsimd", lambda e, e_=e_: e.dma_start(
                out=eo_in[e_ * DHD:(e_ + 1) * DHD, :], in_=xhb[0:DHD, :]),
                reads=["xhb"], writes=["eo_in"], tag="steo")
        pr.coll(lambda e: e.collective_compute(
            "AllGather", ALU.bypass, replica_groups=[list(range(NCORES))],
            ins=[eo_in.opt()], outs=[eo_out.opt()]),
            reads=["eo_in"], writes=["eo_out"], tag="ag_eo")

        pr.fence(["hh", "ct"] + VKEYS, tag="fct")
        for c in range(8):
            pr.dma("sync", lambda e, c=c: e.dma_start(
                out=A32[:, c * 2048:(c + 1) * 2048][:, 0:T], in_=ct_dr[c]),
                reads=["ct_dr", "ct"], writes=["ct"], tag="ldct")
        pr.fence(["q_g", "k_g", "eout"], tag="feo")
        for c in range(8):
            pr.dma("sync", lambda e, c=c: e.dma_start(
                out=B16[:, c * 1024:(c + 1) * 1024], in_=eo_out[c * P:(c + 1) * P, :]),
                reads=["eo_out", "eout"], writes=["eout"], tag="ldeo")
        bcast_fill(2, orows_d[0:1, :], "bc2", tag="gf")
        bcast_fill(3, orows_d[1:2, :], "bc3", tag="bf")
        x0b = wvb.bitcast(F32)[:, 0:DM]
        o_sb = wvb.bitcast(F32)[:, DM:2 * DM]
        pr.fence(["x0b", "o_sb", "wvb"], tag="fx0")
        for tc in range(NCH):
            for half in range(2):
                for c in range(8):
                    pr.t(lambda e, tc=tc, half=half, c=c: e.matmul(
                        pb[half][:, :], A32[:, c * 2048 + tc * P: c * 2048 + (tc + 1) * P],
                        B16[:, c * 1024 + half * 512: c * 1024 + (half + 1) * 512],
                        start=(c == 0), stop=(c == 7)),
                        reads=["ct", "eout"], writes=[f"pb{half}"], tag="mixMM")
            for half in range(2):
                pr.s(lambda e, half=half: e.activation(
                    scrw[:, half * 512:(half + 1) * 512], pb[half][:, :], AF.Silu),
                    reads=[f"pb{half}", "scrw"], writes=["scrw"], tag="silu")
            ln_tile(tc, 2, 3, "bc2", "bc3", tag="lnf")
            pr.dma("sync", lambda e, tc=tc: e.dma_start(out=x0b, in_=x_d[tc * P:(tc + 1) * P, :]),
                   reads=[], writes=["x0b"], tag="ldx0")
            pr.v(lambda e, tc=tc: e.tensor_tensor(o_sb, st(tc), scrw[:, :], ALU.mult),
                 reads=[f"st{tc}", "scrw", "o_sb"], writes=["o_sb"], tag="sg_mul")
            pr.v(lambda e: e.tensor_tensor(o_sb, o_sb, x0b, ALU.add),
                 reads=["o_sb", "x0b"], writes=["o_sb"], tag="o_add")
            pr.dma("sync", lambda e, tc=tc: e.dma_start(out=out_d[tc * P:(tc + 1) * P, :], in_=o_sb),
                   reads=["o_sb"], writes=["outd"], tag="stout")
        pr.op("sync", lambda e: e.nop(nofuse=True), reads=["outd", "dbgk"], tag="final")

        block = ctx.enter_context(nc.Block())
        pr.run(ctx, block)
    return nc



def _pack_layer(Wq, Wk, Wv, Wo, W1, W2):
    out = np.empty(LAYER_ELEMS, np.float32)
    o = 0

    def put(a):
        nonlocal o
        f = a.reshape(-1)
        out[o:o + f.size] = f
        o += f.size
    for c in range(8):
        put(Wq[:, c * P:(c + 1) * P].reshape(8, P, P).transpose(1, 0, 2))
    for c in range(8):
        put(Wk[:, c * P:(c + 1) * P].reshape(8, P, P).transpose(1, 0, 2))
    for h in range(2):
        put(Wv[:, h * 512:(h + 1) * 512].reshape(8, P, 512).transpose(1, 0, 2))
    for dm8 in range(8):
        put(Wo[dm8 * P:(dm8 + 1) * P, :])
    for fc in range(FCN):
        put(W1[:, fc * P:(fc + 1) * P].reshape(8, P, P).transpose(1, 0, 2))
    for fc in range(FCN):
        put(W2[fc * P:(fc + 1) * P, :])
    assert o == LAYER_ELEMS
    return out


def make_core_inputs(x_full, w, T=2048, NL=6):
    Ttot = x_full.shape[0]
    assert Ttot == NCORES * T
    packed = np.stack([_pack_layer(w["Wq"][l], w["Wk"][l], w["Wv"][l],
                                   w["Wo"][l], w["W1"][l], w["W2"][l])
                       for l in range(NL)])
    shards = packed.reshape(NL, NCORES, P, SHARD_COLS)
    bq = np.stack([w["bq"][l].reshape(8, P).T for l in range(NL)])
    bk = np.stack([w["bk"][l].reshape(8, P).T for l in range(NL)])
    b1c_in = np.stack([w["b1"][l].reshape(32, P).T for l in range(NL)])
    rows = np.stack([np.stack([w["bv"][l], w["bo"][l], w["b2"][l],
                               w["g1"][l], w["be1"][l], w["g2"][l], w["be2"][l]])
                     for l in range(NL)])
    orows = np.zeros((8, DM), np.float32)
    orows[0] = w["gf"]
    orows[1] = w["bf"]
    orows[2] = w["norm_g"]
    orows[3] = w["norm_b"]
    orows[4] = w["slot_g"]
    orows[5] = w["slot_b"]
    sed = w["slot_embeds"].reshape(ES, DM)
    maskU = np.triu(np.ones((P, P), np.float32))
    identm = np.eye(P, dtype=np.float32)
    ins = []
    for r in range(NCORES):
        rmask = np.zeros((P, NCORES), np.float32)
        rmask[:, :r] = 1.0
        e0 = E_LOC * r
        ins.append(dict(
            x=np.ascontiguousarray(x_full[r * T:(r + 1) * T]),
            wsh=np.ascontiguousarray(shards[:, r]),
            bq=bq, bk=bk, b1cx=b1c_in, rows=rows, orows=orows,
            scalex=w["scale"].reshape(1, 1),
            sed=sed,
            wgx=np.ascontiguousarray(w["Wg"][e0:e0 + E_LOC].reshape(E_LOC, 8, P, 2048)),
            wux=np.ascontiguousarray(w["Wu"][e0:e0 + E_LOC].reshape(E_LOC, 8, P, 2048)),
            wdx=np.ascontiguousarray(w["Wd"][e0:e0 + E_LOC].reshape(E_LOC, 16, P, DM)),
            bgx=np.ascontiguousarray(w["bg"][e0:e0 + E_LOC]),
            bux=np.ascontiguousarray(w["bu"][e0:e0 + E_LOC]),
            bdx=np.ascontiguousarray(w["bd"][e0:e0 + E_LOC]),
            maskU=maskU, identx=identm, rmaskx=rmask,
        ))
    return ins


def kernel(L0_attn_x, L0_attn_Wq, L0_attn_bq, L0_attn_Wk, L0_attn_bk,
           L0_attn_Wv, L0_attn_bv, L0_attn_Wo, L0_attn_bo, L0_attn_g1,
           L0_attn_be1, L0_attn_W1, L0_attn_b1, L0_attn_W2, L0_attn_b2,
           L0_attn_g2, L0_attn_be2, L0_attn_gf, L0_attn_bf, L0_ffn_norm_g,
           L0_ffn_norm_b, L0_ffn_slot_g, L0_ffn_slot_b, L0_ffn_scale,
           L0_ffn_slot_embeds, L0_ffn_Wg, L0_ffn_bg, L0_ffn_Wu, L0_ffn_bu,
           L0_ffn_Wd, L0_ffn_bd):
    f = lambda a: np.asarray(a, dtype=np.float32)
    T, NL = 2048, 6
    x = f(L0_attn_x)[0]
    xu = np.ascontiguousarray(x[::-1])
    w = dict(Wq=f(L0_attn_Wq), bq=f(L0_attn_bq), Wk=f(L0_attn_Wk), bk=f(L0_attn_bk),
             Wv=f(L0_attn_Wv), bv=f(L0_attn_bv), Wo=f(L0_attn_Wo), bo=f(L0_attn_bo),
             g1=f(L0_attn_g1), be1=f(L0_attn_be1), W1=f(L0_attn_W1), b1=f(L0_attn_b1),
             W2=f(L0_attn_W2), b2=f(L0_attn_b2), g2=f(L0_attn_g2), be2=f(L0_attn_be2),
             gf=f(L0_attn_gf), bf=f(L0_attn_bf), norm_g=f(L0_ffn_norm_g),
             norm_b=f(L0_ffn_norm_b), slot_g=f(L0_ffn_slot_g), slot_b=f(L0_ffn_slot_b),
             scale=f(L0_ffn_scale), slot_embeds=f(L0_ffn_slot_embeds),
             Wg=f(L0_ffn_Wg), bg=f(L0_ffn_bg), Wu=f(L0_ffn_Wu), bu=f(L0_ffn_bu),
             Wd=f(L0_ffn_Wd), bd=f(L0_ffn_bd))
    ins = make_core_inputs(xu, w, T=T, NL=NL)
    nc = build_program(T=T, NL=NL)
    res = run_bass_kernel_spmd(nc, ins, core_ids=list(range(NCORES)))
    out_u = np.concatenate([np.asarray(res.results[r]["out"]) for r in range(NCORES)], axis=0)
    return np.ascontiguousarray(out_u[::-1])[None, :, :]
